# version 12
# baseline (speedup 1.0000x reference)
import numpy as np
import contextlib
import concourse.bass as bass
import concourse.mybir as mybir
from concourse.bass_utils import run_bass_kernel_spmd

F32 = mybir.dt.float32
BF16 = mybir.dt.bfloat16
AF = mybir.ActivationFunctionType
ALU = mybir.AluOpType
AX = mybir.AxisListType

D = 4096
DC = 32
R = 1024
NT = 8
H = 32
HD = 128
BS = 256
NBLK = 4
EPS = 1e-5

A_OFF = 0
B_OFF = 32768
C_OFF = 65536
POOL_BF = C_OFF + 16896
WQK0 = C_OFF + 0
WQK1 = C_OFF + 2048
QB0 = C_OFF + 4096
QB1 = C_OFF + 5120
KF0 = C_OFF + 6144
KF1 = C_OFF + 7168
KT0 = C_OFF + 8192
KT1 = C_OFF + 9216
VT0 = C_OFF + 10240
DD0 = C_OFF + 12288
DD1 = C_OFF + 12800
SDT_O = C_OFF + 13312
KVB_O = C_OFF + 13824
IDT_O = C_OFF + 13952
TRH_O = C_OFF + 14080
KVB1_O = C_OFF + 16128
SDT1_O = C_OFF + 16256
X1T0 = C_OFF + 0
X1T1 = C_OFF + 4096
NW_O = C_OFF + 8192
WOCH0 = C_OFF + 0
WOCH1 = C_OFF + 8192
GST0 = C_OFF + 16384
GST1 = C_OFF + 16640
WD_O = C_OFF + 0
WPN0 = C_OFF + 8192
WPN1 = C_OFF + 10240
WPN2 = C_OFF + 12288
WPN3 = C_OFF + 14336

XF1 = 0
XF0H = 4096
KVA_O = 6144
QDKD_O = 10240
ST_O = 10368
POOL_F32 = 10432


def _decay_tables():
    try:
        import jax
        import jax.numpy as jnp
        with jax.default_device(jax.devices("cpu")[0]):
            base = 1.0 / 2.0 ** (8.0 / H)
            exponent = jnp.arange(H, dtype=jnp.float32) + 1.0
            factor = 1.0 - 1 / (H - 1 + 1e-05) + 1e-05
            slope = (base ** exponent * factor)[:, None, None]
            r = jnp.arange(BS, dtype=jnp.float32) + 1.0
            q_decay = jnp.exp(-slope * r[:, None])
            k_decay = jnp.exp(-slope * (BS - r)[:, None])
            diff = r[:, None] - r[None, :]
            diag = slope[None] * diff[None, None]
            diag = jnp.where(diag >= 0, -diag, -jnp.inf)
            diag_decay = jnp.exp(diag)
            block_decay = jnp.exp(-slope * BS)
            qd = np.asarray(q_decay)[:, :, 0].astype(np.float32)
            kd = np.asarray(k_decay)[:, :, 0].astype(np.float32)
            dd = np.asarray(diag_decay)[0].astype(np.float32)
            bd = np.asarray(block_decay)[:, 0, 0].astype(np.float64)
            return qd, kd, dd, [float(x) for x in bd]
    except Exception:
        pass
    base = np.float32(1.0 / 2.0 ** (8.0 / H))
    exponent = np.arange(H, dtype=np.float32) + np.float32(1.0)
    factor = np.float32(1.0 - 1 / (H - 1 + 1e-5) + 1e-5)
    slope = (base ** exponent) * factor
    r = np.arange(BS, dtype=np.float32) + 1.0
    qd = np.exp(-slope[:, None] * r[None, :]).astype(np.float32)
    kd = np.exp(-slope[:, None] * (BS - r)[None, :]).astype(np.float32)
    diff = r[:, None] - r[None, :]
    dmat = -slope[:, None, None] * diff[None, :, :]
    dmat = np.where(diff[None] >= 0, dmat, -np.inf)
    dd = np.exp(dmat.astype(np.float32)).astype(np.float32)
    bd = np.exp(-slope * np.float32(BS)).astype(np.float64)
    return qd, kd, dd, [float(x) for x in bd]


class _Sem:
    __slots__ = ("name", "h", "n", "eng", "step")

    def __init__(self, name, eng, step):
        self.name, self.h, self.n, self.eng, self.step = name, None, 0, eng, step


class _Ev:
    __slots__ = ("sem", "val")

    def __init__(self, sem, val):
        self.sem, self.val = sem, val


class _Prog:

    def __init__(self):
        self.ops = {"T": [], "V": [], "S": [], "Y": [], "G": []}
        self.sems = []
        self.base = {}

    def sem(self, name, eng, step=1):
        s = _Sem(name, eng, step)
        self.sems.append(s)
        if eng in ("V", "S") and eng not in self.base:
            self.base[eng] = s
        return s

    def emit(self, eng, fn, waits=(), inc=None):
        w = [ww for ww in waits if ww is not None]
        if eng in ("V", "S"):
            b = self.base[eng]
            if b.n > 0:
                w.append(_Ev(b, b.n))
            if inc is None:
                inc = b
        w = tuple((ww.sem, ww.val) for ww in w)
        self.ops[eng].append((w, fn, inc))
        if inc is not None:
            inc.n += inc.step
            ev = _Ev(inc, inc.n)
            self.last_ev = getattr(self, "last_ev", {})
            self.last_ev[eng] = ev
            return ev
        return None

    def mark(self, eng, sem):
        w, fn, inc = self.ops[eng][-1]
        if inc is None:
            sem.n += sem.step
            self.ops[eng][-1] = (w, fn, sem)
            ev = _Ev(sem, sem.n)
            self.last_ev = getattr(self, "last_ev", {})
            self.last_ev[eng] = ev
            return ev
        return self.last_ev[eng]


def _build_program(stop=99, stop2_hp=15, fill_tiles=NT, fill_ln=True, fill_tr=True,
                   och_ocs=16):
    qd_t, kd_t, dd_t, bd_t = _decay_tables()
    nc = bass.Bass("TRN2", target_bir_lowering=False)
    P = _Prog()
    _build_program.last_prog = P

    xm_d = nc.dram_tensor("xm", [NT, 128, D], F32, kind="ExternalInput").ap()
    xp_d = nc.dram_tensor("xp", [NT, 128, D], F32, kind="ExternalInput").ap()
    wqk_d = nc.dram_tensor("wqk", [128, 128, 2048], F32, kind="ExternalInput").ap()
    wv_d = nc.dram_tensor("wv", [64, 128, 2048], F32, kind="ExternalInput").ap()
    wg_d = nc.dram_tensor("wg", [64, 128, 2048], F32, kind="ExternalInput").ap()
    wo_d = nc.dram_tensor("wo", [64, 128, 2048], F32, kind="ExternalInput").ap()
    wp_d = nc.dram_tensor("wp", [64, 128, 2048], F32, kind="ExternalInput").ap()
    wn_d = nc.dram_tensor("wn", [64, 128, 2048], F32, kind="ExternalInput").ap()
    nwb_d = nc.dram_tensor("nwb", [128, D], F32, kind="ExternalInput").ap()
    ddt_d = nc.dram_tensor("ddt", [H, 128, 2 * BS], F32, kind="ExternalInput").ap()
    qdkd_d = nc.dram_tensor("qdkd", [128, 128], F32, kind="ExternalInput").ap()
    idin_d = nc.dram_tensor("idin", [128, 128], F32, kind="ExternalInput").ap()
    out_d = nc.dram_tensor("out", [NT, 128, D], F32, kind="ExternalOutput").ap()
    xres_d = nc.dram_tensor("xres", [NT, 128, D], F32, kind="Internal").ap()
    xre2_d = nc.dram_tensor("xre2", [NT, 128, D], F32, kind="Internal").ap()

    stack = contextlib.ExitStack()
    pb_h = stack.enter_context(nc.sbuf_tensor("pb", [128, POOL_BF], BF16))
    pf_h = stack.enter_context(nc.sbuf_tensor("pf", [128, POOL_F32], F32))
    PS = {}
    for k in "ABCDEF":
        PS[k] = stack.enter_context(nc.psum_tensor("ps" + k, [128, 512], F32))
    PT = [stack.enter_context(nc.psum_tensor("pt0", [128, 512], BF16)),
          stack.enter_context(nc.psum_tensor("pt1", [128, 512], BF16))]

    def bf(off, n):
        return pb_h[:, off:off + n]

    def f32(off, n):
        return pf_h[:, off:off + n]

    def stc(i):
        return pf_h[:, ST_O + i:ST_O + i + 1]

    IDT = bf(IDT_O, 128)
    A3 = pb_h[:, A_OFF:A_OFF + 32768].rearrange("p (c t) -> p c t", t=R)
    PT3 = [PT[i][:].rearrange("p (a t) -> p a t", t=128) for i in range(2)]

    sT = P.sem("sT", "T")
    sV = P.sem("sV", "V")
    sS = P.sem("sS", "S")
    yA = P.sem("yA", "Y", 16)
    yA1 = P.sem("yA1", "Y", 16)
    yB = P.sem("yB", "Y", 16)
    yB2 = P.sem("yB2", "Y", 16)
    yD = P.sem("yD", "Y", 16)
    yI = P.sem("yI", "Y", 16)
    gN = P.sem("gN", "G", 16)
    g0 = P.sem("g0", "G", 16)
    g1 = P.sem("g1", "G", 16)
    g2 = P.sem("g2", "G", 16)
    g3 = P.sem("g3", "G", 16)
    g4 = P.sem("g4", "G", 16)
    g5 = P.sem("g5", "G", 16)
    gI = P.sem("gI", "G", 16)

    bank_free = {}
    free_ev = {}

    def mm(ps_ap, lhsT, rhs, start, stop, bank, waits=(), inc=None):
        w = list(waits)
        if start:
            w.append(bank_free.get(bank))
        return P.emit("T", lambda e, o=ps_ap, l=lhsT, r=rhs, st=start, sp=stop:
                      e.matmul(o, l, r, start=st, stop=sp), waits=w, inc=inc)

    def mark(eng):
        s = {"T": sT, "V": sV, "S": sS}[eng]
        return P.mark(eng, s)

    ev_qdkd = P.emit("Y", lambda e: e.dma_start(out=f32(QDKD_O, 128), in_=qdkd_d), inc=yI)
    ev_ident = [P.emit("G", lambda e: e.dma_start(out=IDT, in_=idin_d), inc=gI)]

    def qd_ap(h, qc):
        return pf_h[:, QDKD_O + h * 2 + qc: QDKD_O + h * 2 + qc + 1]

    def kd_ap(h, kc):
        return pf_h[:, QDKD_O + 64 + h * 2 + kc: QDKD_O + 64 + h * 2 + kc + 1]

    stats_busy = [None]

    def rstd_refine(col_var, col_out, eps, waits):
        evv = P.emit("V", lambda e: e.tensor_scalar_add(stc(8), col_var, float(eps)),
                     waits=waits, inc=sV)
        evs = P.emit("S", lambda e: e.activation(stc(9), stc(8), AF.Sqrt), waits=[evv], inc=sS)
        P.emit("V", lambda e: e.reciprocal(stc(10), stc(9)), waits=[evs])
        for _ in range(2):
            P.emit("V", lambda e: e.tensor_mul(stc(11), stc(10), stc(10)))
            P.emit("V", lambda e: e.tensor_mul(stc(11), stc(11), stc(8)))
            P.emit("V", lambda e: e.tensor_scalar(stc(11), stc(11), -0.5, 1.5, ALU.mult, ALU.add))
            P.emit("V", lambda e: e.tensor_mul(stc(10), stc(11), stc(10)))
        return P.emit("V", lambda e, co=col_out: e.tensor_copy(co, stc(10)), inc=sV)

    def row_sumsq(src_ap, out_col, waits):
        n = src_ap.shape[-1]
        n2 = n // 2
        e1 = P.emit("S", lambda e, s=src_ap: e.activation(bf(TRH_O, n2), s[:, 0:n2], AF.Square,
                                                          accum_out=stc(12)), waits=waits, inc=sS)
        e2 = P.emit("S", lambda e, s=src_ap: e.activation(bf(TRH_O, n2), s[:, n2:n], AF.Square,
                                                          accum_out=stc(13)), waits=waits, inc=sS)
        return P.emit("V", lambda e, oc=out_col: e.tensor_add(oc, stc(12), stc(13)),
                      waits=[e1, e2], inc=sV)

    def row_stats(src_ap, mean_col, var_col, waits):
        n = src_ap.shape[-1]
        ev_s = P.emit("V", lambda e, s=src_ap: e.reduce_sum(stc(14), s, axis=AX.X),
                      waits=waits, inc=sV)
        ev_q = row_sumsq(src_ap, stc(15), waits)
        P.emit("V", lambda e, mc=mean_col: e.tensor_scalar_mul(mc, stc(14), 1.0 / n), waits=[ev_s])
        P.emit("V", lambda e: e.tensor_scalar_mul(stc(15), stc(15), 1.0 / n), waits=[ev_q])
        P.emit("V", lambda e, mc=mean_col: e.tensor_mul(stc(14), mc, mc))
        return P.emit("V", lambda e, vc=var_col: e.tensor_sub(vc, stc(15), stc(14)), inc=sV)

    def transpose_tile_to_A(src_off, tt, src_ready, extra_first_waits=()):
        last = None
        for cg in range(8):
            pi = cg % 2
            evt = None
            for ci in range(4):
                c = cg * 4 + ci
                w = [src_ready, ev_ident[-1]]
                if ci == 0:
                    w.append(bank_free.get(("pt", pi)))
                w += list(extra_first_waits) if (cg == 0 and ci == 0) else []
                evt = P.emit("T", lambda e, cc=c, b=pi, ii=ci, so=src_off: e.transpose(
                    PT[b][:, ii * 128:(ii + 1) * 128],
                    bf(so + cc * 128, 128), IDT),
                    waits=w, inc=sT if ci == 3 else None)
            ev_ev = P.emit("S", lambda e, cg_=cg, t=tt, b=pi: e.activation(
                A3[:, cg_ * 4:cg_ * 4 + 4, t * 128:(t + 1) * 128],
                PT3[b][:, 0:4, :], AF.Copy),
                waits=[evt], inc=sS)
            bank_free[("pt", pi)] = ev_ev
            last = ev_ev
        return last

    def emit_fill_A(src_d, a_readers_done, slots=(XF1,)):
        last = None
        for tt in range(fill_tiles):
            sl = tt % 2
            xfo = slots[tt % len(slots)]
            ev_ld = P.emit("Y", lambda e, t=tt, o=xfo: e.dma_start(
                out=f32(o, 4096), in_=src_d[t]),
                waits=[free_ev.get(("xf", xfo))], inc=(yA if xfo == XF1 else yA1))
            if not fill_ln:
                last = ev_ld
                free_ev[("xf", xfo)] = ev_ld
                continue
            xf = f32(xfo, 4096)
            ev_var = row_stats(xf, stc(0), stc(1), [ev_ld, stats_busy[0]])
            ev_rstd = rstd_refine(stc(1), stc(2), EPS, [ev_var])
            ev_ap = P.emit("V", lambda e, s=sl, o=xfo: e.tensor_scalar(
                bf(X1T0 + s * 4096, 4096), f32(o, 4096), stc(0), stc(2),
                ALU.subtract, ALU.mult),
                waits=[free_ev.get(("x1t", sl)), ev_rstd], inc=sV)
            stats_busy[0] = ev_ap
            free_ev[("xf", xfo)] = ev_ap
            if not fill_tr:
                last = ev_ap
                free_ev[("x1t", sl)] = ev_ap
                continue
            last = transpose_tile_to_A(X1T0 + sl * 4096, tt, ev_ap,
                                       extra_first_waits=[a_readers_done])
            free_ev[("x1t", sl)] = last
        return last

    wslot_i = [0]
    wsem = [g0, g1]

    def load_wslot(src_ap):
        i = wslot_i[0] % 2
        wslot_i[0] += 1
        ev = P.emit("G", lambda e, ii=i, s=src_ap: e.dma_start(
            out=bf(WQK0 + ii * 2048, 2048), in_=s),
            waits=[free_ev.get(("wslot", i))], inc=wsem[i])
        return i, ev

    def proj_fm(g, banks, a_ready):
        fills = [None, None]
        for half in range(2):
            sl, ev_ld = load_wslot(wqk_d[2 * g + half])
            last = None
            for dkl in range(16):
                dk = half * 16 + dkl
                for th in range(2):
                    last = mm(PS[banks[th]][:, 0:512],
                              bf(WQK0 + sl * 2048 + dkl * 128, 128),
                              bf(A_OFF + dk * R + th * 512, 512),
                              start=(dk == 0), stop=(dk == 31), bank=banks[th],
                              waits=[ev_ld, a_ready],
                              inc=sT if dkl == 15 else None)
                    if dk == 31:
                        fills[th] = last
            free_ev[("wslot", sl)] = last
        return fills

    def emit_headpair(hp, prefix, a_ready, kv_init_ev):
        for hh in range(2):
            h = 2 * hp + hh
            hs = h % 2
            qb = QB0 if hs == 0 else QB1
            kf = KF0 if hs == 0 else KF1
            kt = KT0 if hs == 0 else KT1
            ddo = DD0 if hs == 0 else DD1
            if not prefix:
                ev_dd = P.emit("G", lambda e, h_=h, d_=ddo: e.dma_start(
                    out=bf(d_, 512), in_=ddt_d[h_]),
                    waits=[free_ev.get(("dd", hs))], inc=(g2 if hs == 0 else g3))
                free_ev[("ddld", h)] = ev_dd
                fills = proj_fm(h, ("A", "B"), a_ready)
                for th in range(2):
                    ev = P.emit("S", lambda e, t=th, q=qb, b=("A", "B")[th]: e.activation(
                        bf(q + t * 512, 512), PS[b][:, 0:512], AF.Silu),
                        waits=[fills[th], free_ev.get(("qb", hs))], inc=sS)
                    bank_free[("A", "B")[th]] = ev
                    free_ev[("qrdy", h, th)] = ev
            fills = proj_fm(32 + h, ("C", "D"), a_ready)
            for th in range(2):
                ev = P.emit("S", lambda e, t=th, k=kf, b=("C", "D")[th]: e.activation(
                    bf(k + t * 512, 512), PS[b][:, 0:512], AF.Silu),
                    waits=[fills[th], free_ev.get(("kf", hs))], inc=sS)
                bank_free[("C", "D")[th]] = ev
                free_ev[("krdy", h, th)] = ev
            for tt in range(NT):
                pi = tt % 2
                w = [free_ev[("krdy", h, tt // 4)], ev_ident[-1], bank_free.get(("pt", pi))]
                if tt < 2:
                    w.append(free_ev.get(("kt", hs)))
                evt = P.emit("T", lambda e, k=kf, t=tt, b=pi: e.transpose(
                    PT[b][:, 0:128],
                    bf(k + t * 128, 128), IDT), waits=w, inc=sT)
                ev_ev = P.emit("S", lambda e, t=tt, k2=kt, b=pi, h_=h: e.activation(
                    bf(k2 + t * 128, 128),
                    PT[b][:, 0:128], AF.Copy,
                    scale=kd_ap(h_, t % 2)),
                    waits=[evt, ev_qdkd], inc=sS)
                bank_free[("pt", pi)] = ev_ev
                free_ev[("ktm", h, tt)] = ev_ev

        vto = VT0
        for p2 in range(2):
            vlast = None
            for sl4 in range(4):
                sl, ev_ld = load_wslot(wv_d[hp * 4 + sl4])
                for a in range(8):
                    dk = sl4 * 8 + a
                    for t4 in range(4):
                        tt = p2 * 4 + t4
                        bk = "ABCD"[t4]
                        vlast = mm(PS[bk][:, 0:256],
                                   bf(A_OFF + dk * R + tt * 128, 128),
                                   bf(WQK0 + sl * 2048 + a * 256, 256),
                                   start=(dk == 0), stop=(dk == 31), bank=bk,
                                   waits=[ev_ld, a_ready],
                                   inc=sT if (a == 7 and t4 == 3) else None)
                free_ev[("wslot", sl)] = vlast
            for t4 in range(4):
                tt = p2 * 4 + t4
                bk = "ABCD"[t4]
                ev = P.emit("S", lambda e, t=tt, b=bk, v=vto: e.activation(
                    bf(v + t * 256, 256), PS[b][:, 0:256], AF.Silu),
                    waits=[vlast, free_ev.get(("vt",))], inc=sS)
                free_ev[("vrdy", hp, tt)] = ev
                bank_free[bk] = ev

        hcfg = []
        for hh in range(2):
            h = 2 * hp + hh
            hs = h % 2
            hcfg.append(dict(
                h=h, hs=hs, hh=hh,
                qb=QB0 if hs == 0 else QB1,
                kf=KF0 if hs == 0 else KF1,
                kt=KT0 if hs == 0 else KT1,
                kv=f32(KVA_O + hp * 256 + hh * 128, 128),
                kvb=KVB_O if hs == 0 else KVB1_O,
                sdt=SDT_O if hs == 0 else SDT1_O,
                dd=DD0 if hs == 0 else DD1,
                last_inter=None, last_state=None))
        for n in range(NBLK):
            if not prefix:
                for c in hcfg:
                    h, hs, hh = c["h"], c["hs"], c["hh"]
                    ev_kvb = P.emit("V", lambda e, k=c["kv"], kb=c["kvb"]: e.tensor_copy(
                        bf(kb, 128), k),
                        waits=[kv_init_ev, free_ev.get(("kvb", hs))], inc=sV)
                    ev_sd = [None, None]
                    for ktc in range(2):
                        tile = 2 * n + ktc
                        bk = "EF"[ktc]
                        evs_ = mm(PS[bk][:, 0:256], bf(c["kf"] + tile * 128, 128),
                                  bf(c["qb"] + n * 256, 256), start=True, stop=True, bank=bk,
                                  waits=[free_ev[("krdy", h, tile // 4)],
                                         free_ev[("qrdy", h, n // 2)]],
                                  inc=sT)
                        ev_sd[ktc] = P.emit("V", lambda e, k2=ktc, b=bk, so=c["sdt"],
                                            do=c["dd"]: e.tensor_tensor(
                            bf(so + k2 * 256, 256), PS[b][:, 0:256],
                            bf(do + k2 * 256, 256), ALU.mult),
                            waits=[evs_, free_ev[("ddld", h)], free_ev.get(("sdt", hs))],
                            inc=sV)
                        bank_free[bk] = ev_sd[ktc]
                    ev_emain = [None, None]
                    for qtc in range(2):
                        tt = 2 * n + qtc
                        bmain = "CD"[qtc]
                        binter = "AB"[qtc]
                        evm = None
                        for ktc in range(2):
                            evm = mm(PS[bmain][:, 0:128],
                                     bf(c["sdt"] + ktc * 256 + qtc * 128, 128),
                                     bf(vto + (2 * n + ktc) * 256 + hh * 128, 128),
                                     start=(ktc == 0), stop=(ktc == 1), bank=bmain,
                                     waits=[ev_sd[ktc], free_ev[("vrdy", hp, 2 * n + ktc)]],
                                     inc=sT if ktc == 1 else None)
                        evi = mm(PS[binter][:, 0:128], bf(c["qb"] + tt * 128, 128),
                                 bf(c["kvb"], 128), start=True, stop=True, bank=binter,
                                 waits=[ev_kvb, free_ev[("qrdy", h, n // 2)]], inc=sT)
                        boff = B_OFF + tt * 4096 + h * 128
                        P.emit("V", lambda e, bo=boff, b=binter, h_=h, q=qtc: e.tensor_scalar_mul(
                            bf(bo, 128), PS[b][:, 0:128], qd_ap(h_, q)),
                            waits=[evi, ev_qdkd])
                        eva = P.emit("V", lambda e, bo=boff, b=bmain: e.tensor_add(
                            bf(bo, 128), bf(bo, 128), PS[b][:, 0:128]),
                            waits=[evm], inc=sV)
                        bank_free[bmain] = eva
                        bank_free[binter] = eva
                        ev_emain[qtc] = eva
                        c["last_inter"] = eva
                    free_ev[("sdt", hs)] = ev_emain[1]
                    free_ev[("kvb", hs)] = ev_emain[1]
            for c in hcfg:
                h, hh = c["h"], c["hh"]
                evst = None
                for ktc in range(2):
                    tile = 2 * n + ktc
                    evst = mm(PS["F"][:, 128:256], bf(c["kt"] + tile * 128, 128),
                              bf(vto + tile * 256 + hh * 128, 128),
                              start=(ktc == 0), stop=(ktc == 1), bank="F",
                              waits=[free_ev[("ktm", h, tile)], free_ev[("vrdy", hp, tile)]],
                              inc=sT if ktc == 1 else None)
                ev_upd = P.emit("V", lambda e, k=c["kv"], b=bd_t[h]: e.scalar_tensor_tensor(
                    k, k, b, PS["F"][:, 128:256], ALU.mult, ALU.add),
                    waits=[evst, kv_init_ev], inc=sV)
                bank_free["F"] = ev_upd
                c["last_state"] = ev_upd
        for c in hcfg:
            hs = c["hs"]
            free_ev[("qb", hs)] = c["last_inter"] if not prefix else None
            free_ev[("kf", hs)] = c["last_inter"] if not prefix else c["last_state"]
            free_ev[("kt", hs)] = c["last_state"]
            free_ev[("dd", hs)] = c["last_inter"] if not prefix else None
            free_ev[("kvdone", c["h"])] = c["last_state"]
        free_ev[("vt",)] = free_ev[("kvdone", 2 * hp + 1)]

    def emit_och_matmul(w_dram, a_ready, evac_fn, first_w=(), first_w_odd=None,
                        name=""):
        grp = [("A", "B", "C"), ("D", "E", "F")]
        last_evs = []
        for oc in range(och_ocs):
            wbuf = WOCH0 if oc % 2 == 0 else WOCH1
            gsem = g2 if oc % 2 == 0 else g3
            fw = list(first_w) if (oc % 2 == 0 or first_w_odd is None) else list(first_w_odd)
            evw = None
            for piece in range(4):
                evw = P.emit("G", lambda e, s=w_dram[oc * 4 + piece], wb=wbuf, pp=piece: e.dma_start(
                    out=bf(wb + pp * 2048, 2048), in_=s),
                    waits=[free_ev.get(("woch", oc % 2))] + (fw if piece == 0 else []),
                    inc=gsem)
            banks = grp[oc % 2]
            lastmm = None
            for tt in range(NT):
                bk = banks[tt % 3]
                for dk in range(DC):
                    lastmm = mm(PS[bk][:, 0:256],
                                bf(A_OFF + dk * R + tt * 128, 128),
                                bf(wbuf + (dk // 8) * 2048 + (dk % 8) * 256, 256),
                                start=(dk == 0), stop=(dk == DC - 1),
                                bank=(bk if dk == 0 else None),
                                waits=[evw, a_ready],
                                inc=sT if dk == DC - 1 else None)
                ev = evac_fn(oc, tt, PS[bk][:, 0:256], [lastmm])
                bank_free[bk] = ev
                last_evs.append(ev)
            free_ev[("woch", oc % 2)] = lastmm
        return last_evs[-1]

    def _materialize():
        for s in P.sems:
            s.h = stack.enter_context(nc.semaphore(s.name))

        with nc.Block() as block:
            def runner(engname):
                def run(e):
                    hw = {}
                    for waits, fn, inc in P.ops[engname]:
                        for sem, val in waits:
                            if hw.get(sem.name, 0) < val:
                                e.wait_ge(sem.h, val)
                                hw[sem.name] = val
                        inst = fn(e)
                        if inc is not None:
                            inst.then_inc(inc.h, inc.step)
                return run

            block.tensor(runner("T"))
            block.vector(runner("V"))
            block.scalar(runner("S"))
            block.sync(runner("Y"))
            block.gpsimd(runner("G"))

        stack.close()
        return nc

    def _early_out(last_ev):
        for tt in range(NT):
            P.emit("Y", lambda e, t=tt: e.dma_start(out=out_d[t], in_=f32(XF0, 4096)),
                   waits=[last_ev], inc=yD)
        P.emit("G", lambda e: e.wait_ge(yD.h, yD.n), waits=())
        return _materialize()

    ev_a1 = emit_fill_A(xp_d, None, slots=(XF1, KVA_O))
    if stop <= 1:
        return _early_out(ev_a1)
    free_ev[("wslot", 0)] = ev_a1
    free_ev[("wslot", 1)] = ev_a1
    prefix_done = {}
    for hp in range(16):
        ev_init = P.emit("V", lambda e, o=KVA_O + hp * 256: e.memset(f32(o, 256), 0.0),
                         inc=sV)
        emit_headpair(hp, prefix=True, a_ready=ev_a1, kv_init_ev=ev_init)
        prefix_done[hp] = free_ev[("kvdone", 2 * hp + 1)]
        if stop <= 2 and hp >= stop2_hp:
            break
    ev_prefix_T = mark("T")
    if stop <= 2:
        return _early_out(mark("V"))

    ev_a2 = emit_fill_A(xm_d, ev_prefix_T)
    if stop <= 3:
        return _early_out(ev_a2)
    free_ev[("wslot", 0)] = ev_a2
    free_ev[("wslot", 1)] = ev_a2

    for hp in range(16):
        emit_headpair(hp, prefix=False, a_ready=ev_a2, kv_init_ev=prefix_done[hp])
        if stop <= 4 and hp >= stop2_hp:
            break
    ev_attn_T = mark("T")
    ev_attn_V = mark("V")
    if stop <= 4:
        return _early_out(ev_attn_V)

    ev_nw = P.emit("G", lambda e: e.dma_start(out=bf(NW_O, 4096), in_=nwb_d),
                   waits=[ev_attn_T], inc=gN)
    ev_norm_last = None
    for tt in range(NT):
        bt = bf(B_OFF + tt * 4096, 4096)
        ev_q = row_sumsq(bt, stc(0), [ev_attn_V, stats_busy[0]])
        P.emit("V", lambda e: e.tensor_scalar_mul(stc(0), stc(0), 1.0 / D), waits=[ev_q])
        ev_r = rstd_refine(stc(0), stc(1), EPS, [])
        ev1 = P.emit("V", lambda e, b=bt: e.tensor_scalar_mul(b, b, stc(1)), waits=[ev_r], inc=sV)
        stats_busy[0] = ev1
        ev_norm_last = P.emit("V", lambda e, b=bt: e.tensor_mul(b, b, bf(NW_O, 4096)),
                              waits=[ev_nw], inc=sV)
    if stop <= 5:
        return _early_out(ev_norm_last)

    def gate_evac(oc, tt, ps_ap, waits):
        gsl = GST0 if tt % 2 == 0 else GST1
        evs_ = P.emit("S", lambda e, g=gsl, p=ps_ap: e.activation(bf(g, 256), p, AF.Sigmoid),
                      waits=waits + [free_ev.get(("gst", tt % 2)), ev_norm_last], inc=sS)
        bo = B_OFF + tt * 4096 + oc * 256
        ev = P.emit("V", lambda e, b=bo, g=gsl: e.tensor_mul(bf(b, 256), bf(b, 256), bf(g, 256)),
                    waits=[evs_], inc=sV)
        free_ev[("gst", tt % 2)] = ev
        return ev

    ev_gate_last = emit_och_matmul(wg_d, ev_a2, gate_evac,
                                   first_w=[ev_attn_T],
                                   first_w_odd=[ev_attn_T, ev_norm_last], name="gate")
    ev_gate_T = mark("T")
    if stop <= 6:
        return _early_out(ev_gate_last)

    ev_ident.append(P.emit("G", lambda e: e.dma_start(out=IDT, in_=idin_d),
                           waits=[ev_gate_T], inc=gI))
    ev_ng = None
    for tt in range(NT):
        ev_ng = transpose_tile_to_A(B_OFF + tt * 4096, tt, ev_gate_last,
                                    extra_first_waits=[ev_gate_T])
    ev_ng_T = mark("T")
    if stop <= 7:
        return _early_out(ev_ng)

    def y_evac(oc, tt, ps_ap, waits):
        bo = B_OFF + tt * 4096 + oc * 256
        return P.emit("S", lambda e, b=bo, p=ps_ap: e.activation(bf(b, 256), p, AF.Copy),
                      waits=waits + [ev_ng_T if oc == 0 else None], inc=sS)

    ev_y_last = emit_och_matmul(wo_d, ev_ng, y_evac, first_w=[ev_ng_T], name="out")
    ev_out_T = mark("T")
    if stop <= 8:
        return _early_out(ev_y_last)

    TMP = f32(KVA_O, 4096)
    for tt in range(NT):
        bt = bf(B_OFF + tt * 4096, 4096)
        ev_var = row_stats(bt, stc(0), stc(1), [ev_y_last, stats_busy[0]])
        ev_r = rstd_refine(stc(1), stc(2), EPS, [ev_var])
        ev_t = P.emit("V", lambda e, b=bt: e.tensor_scalar(
            f32(XF1, 4096), b, stc(0), stc(2), ALU.subtract, ALU.mult),
            waits=[ev_r, free_ev.get(("xf", XF1))], inc=sV)
        ev_hi = None
        for h2 in range(2):
            off = h2 * 2048
            ev_ldx = P.emit("Y", lambda e, t=tt, o=off: e.dma_start(
                out=f32(XF0H, 2048), in_=xm_d[t][:, o:o + 2048]),
                waits=[free_ev.get(("xf8h",)) or free_ev.get(("xf", XF1))], inc=yA)
            ev_hi = P.emit("V", lambda e, o=off: e.tensor_add(
                pf_h[:, KVA_O + o:KVA_O + o + 2048], f32(XF1 + o, 2048), f32(XF0H, 2048)),
                waits=[ev_ldx, ev_t, free_ev.get(("xfhi",)) if h2 == 0 else None], inc=sV)
            P.emit("V", lambda e, o=off: e.tensor_sub(
                f32(XF0H, 2048), f32(XF0H, 2048), pf_h[:, KVA_O + o:KVA_O + o + 2048]),
                inc=sV)
            ev_lo = P.emit("V", lambda e, o=off: e.tensor_add(
                f32(XF0H, 2048), f32(XF0H, 2048), f32(XF1 + o, 2048)), inc=sV)
            ev_sl = P.emit("Y", lambda e, t=tt, o=off: e.dma_start(
                out=xre2_d[t][:, o:o + 2048], in_=f32(XF0H, 2048)),
                waits=[ev_lo], inc=yB2)
            free_ev[("xf8h",)] = ev_sl
        free_ev[("xre2", tt)] = free_ev[("xf8h",)]
        ev_sh = P.emit("Y", lambda e, t=tt: e.dma_start(out=xres_d[t], in_=TMP),
                       waits=[ev_hi], inc=yB)
        free_ev[("xres", tt)] = ev_sh
        free_ev[("xfhi",)] = ev_sh
        ev_var2 = row_stats(TMP, stc(3), stc(4), [ev_hi])
        ev_r2 = rstd_refine(stc(4), stc(5), EPS, [ev_var2])
        ev_x2 = P.emit("V", lambda e, b=bt: e.tensor_scalar(
            b, TMP, stc(3), stc(5), ALU.subtract, ALU.mult),
            waits=[ev_r2], inc=sV)
        stats_busy[0] = ev_x2
        free_ev[("x2", tt)] = ev_x2
    ev_x2_all = mark("V")
    if stop <= 9:
        return _early_out(ev_x2_all)

    ev_ident.append(P.emit("G", lambda e: e.dma_start(out=IDT, in_=idin_d),
                           waits=[ev_out_T], inc=gI))
    ev_x2fm = None
    for tt in range(NT):
        ev_x2fm = transpose_tile_to_A(B_OFF + tt * 4096, tt, free_ev[("x2", tt)],
                                      extra_first_waits=[ev_out_T])
    ev_x2fm_T = mark("T")
    if stop <= 10:
        return _early_out(ev_x2fm)

    wpn_sem = {0: g2, 1: g3, 2: g4, 3: g5}
    wd_ready = {}
    for oc in range(16):
        for sl4 in range(4):
            sp = WPN0 if sl4 % 2 == 0 else WPN1
            sn = WPN2 if sl4 % 2 == 0 else WPN3
            idx = sl4 % 2
            evp = P.emit("G", lambda e, s=sp, src=wp_d[oc * 4 + sl4]: e.dma_start(
                out=bf(s, 2048), in_=src),
                waits=[free_ev.get(("wpn", idx)), ev_x2fm_T if (oc == 0 and sl4 == 0) else None],
                inc=wpn_sem[idx])
            evn = P.emit("G", lambda e, s=sn, src=wn_d[oc * 4 + sl4]: e.dma_start(
                out=bf(s, 2048), in_=src),
                waits=[free_ev.get(("wpn2", idx))], inc=wpn_sem[idx + 2])
            ev1 = P.emit("V", lambda e, s=sp, q=sl4: e.tensor_scalar(
                bf(WD_O + q * 2048, 2048), bf(s, 2048), 0.0, None, ALU.is_gt),
                waits=[evp, free_ev.get(("wd", sl4))], inc=sV)
            free_ev[("wpn", idx)] = ev1
            ev2 = P.emit("V", lambda e, s=sn: e.tensor_scalar(
                bf(s, 2048), bf(s, 2048), 0.0, None, ALU.is_gt),
                waits=[evn], inc=sV)
            ev3 = P.emit("V", lambda e, s=sn, q=sl4: e.tensor_sub(
                bf(WD_O + q * 2048, 2048), bf(WD_O + q * 2048, 2048), bf(s, 2048)),
                waits=[ev1, ev2], inc=sV)
            free_ev[("wpn2", idx)] = ev3
            wd_ready[sl4] = ev3
        banks = [("A", "B", "C"), ("D", "E", "F")][oc % 2]
        lastmm = None
        for tt in range(NT):
            bk = banks[tt % 3]
            for dk in range(DC):
                lastmm = mm(PS[bk][:, 0:256],
                            bf(A_OFF + dk * R + tt * 128, 128),
                            bf(WD_O + (dk // 8) * 2048 + (dk % 8) * 256, 256),
                            start=(dk == 0), stop=(dk == DC - 1),
                            bank=(bk if dk == 0 else None),
                            waits=[wd_ready[dk // 8], ev_x2fm if (dk == 0 and tt == 0) else None],
                            inc=sT if dk == DC - 1 else None)
            bo = B_OFF + tt * 4096 + oc * 256
            ev = P.emit("S", lambda e, b=bo, ps=PS[bk]: e.activation(
                bf(b, 256), ps[:, 0:256], AF.Copy),
                waits=[lastmm, ev_x2_all if oc == 0 else None], inc=sS)
            bank_free[bk] = ev
        for q in range(4):
            free_ev[("wd", q)] = lastmm
    ev_ffn_T = mark("T")
    ev_ffn_V = mark("V")
    if stop <= 11:
        return _early_out(ev_ffn_V)

    ph8_x2 = stats_busy[0]
    for tt in range(NT):
        bt = bf(B_OFF + tt * 4096, 4096)
        ev_var = row_stats(bt, stc(0), stc(1), [ev_ffn_V, ev_ffn_T, stats_busy[0]])
        ev_r = rstd_refine(stc(1), stc(2), EPS, [ev_var])
        ev_t = P.emit("V", lambda e, b=bt: e.tensor_scalar(
            f32(XF1, 4096), b, stc(0), stc(2), ALU.subtract, ALU.mult),
            waits=[ev_r, free_ev.get(("xf11", 1))], inc=sV)
        w1 = [free_ev[("xre2", tt)], free_ev.get(("xf11t",))]
        if tt == 0:
            w1 += [free_ev[("xres", NT - 1)], ph8_x2]
        ev_ld2 = P.emit("Y", lambda e, t=tt: e.dma_start(out=TMP, in_=xre2_d[t]),
                        waits=w1, inc=yI)
        ev_o2 = P.emit("V", lambda e: e.tensor_add(f32(XF1, 4096), f32(XF1, 4096), TMP),
                       waits=[ev_ld2, ev_t], inc=sV)
        free_ev[("xf11t",)] = ev_o2
        ev_o = None
        for h2 in range(2):
            off = h2 * 2048
            ev_ldh = P.emit("Y", lambda e, t=tt, o=off: e.dma_start(
                out=f32(XF0H, 2048), in_=xres_d[t][:, o:o + 2048]),
                waits=[free_ev[("xres", tt)], free_ev.get(("xf8h",))], inc=yA)
            ev_o = P.emit("V", lambda e, o=off: e.tensor_add(
                f32(XF1 + o, 2048), f32(XF1 + o, 2048), f32(XF0H, 2048)),
                waits=[ev_ldh, ev_o2], inc=sV)
            free_ev[("xf8h",)] = ev_o
        ev_st = P.emit("Y", lambda e, t=tt: e.dma_start(out=out_d[t], in_=f32(XF1, 4096)),
                       waits=[ev_o], inc=yD)
        free_ev[("xf11", 1)] = ev_st

    P.emit("G", lambda e: e.wait_ge(yD.h, yD.n), waits=())
    return _materialize()


_CACHE = {}


def _pack_och(wT):
    return np.ascontiguousarray(
        wT.reshape(4, 8, 128, 16, 256).transpose(3, 0, 2, 1, 4).reshape(64, 128, 2048))


def _host_prep(inputs):
    hs = np.asarray(inputs["L0_attn_hidden_states"], dtype=np.float32)
    qkv_w = np.asarray(inputs["L0_attn_qkv_w"], dtype=np.float32)
    out_w = np.asarray(inputs["L0_attn_out_w"], dtype=np.float32)
    gate_w = np.asarray(inputs["L0_attn_gate_w"], dtype=np.float32)
    norm_w = np.asarray(inputs["L0_attn_norm_w"], dtype=np.float32)
    w_pos = np.asarray(inputs["L0_ffn_w_pos"], dtype=np.float32)
    w_neg = np.asarray(inputs["L0_ffn_w_neg"], dtype=np.float32)

    wq = np.ascontiguousarray(
        qkv_w.reshape(H, 3, HD, D).transpose(1, 0, 2, 3).reshape(3 * D, D).T)
    wqk = np.ascontiguousarray(
        wq[:, :2 * D].reshape(2, 16, 128, 64, 128).transpose(3, 0, 2, 1, 4).reshape(
            128, 128, 2048))
    wv = _pack_och(wq[:, 2 * D:])
    wg = _pack_och(np.ascontiguousarray(gate_w.T))
    wo = _pack_och(np.ascontiguousarray(out_w.T))
    wp = _pack_och(np.ascontiguousarray(w_pos.T))
    wn = _pack_och(np.ascontiguousarray(w_neg.T))
    nwb = np.ascontiguousarray(np.broadcast_to(norm_w[None, :], (128, D)))

    qd_t, kd_t, dd_t, _bd = _decay_tables()
    ddt = np.ascontiguousarray(
        dd_t.reshape(H, BS, 2, 128).transpose(0, 3, 2, 1).reshape(H, 128, 2 * BS))
    qdkd = np.zeros((128, 128), dtype=np.float32)
    for h in range(H):
        for c in range(2):
            qdkd[:, h * 2 + c] = qd_t[h, c * 128:(c + 1) * 128]
            qdkd[:, 64 + h * 2 + c] = kd_t[h, c * 128:(c + 1) * 128]
    idin = np.eye(128, dtype=np.float32)

    zeros_half = np.zeros((NT, 128, D), dtype=np.float32)
    in_maps = []
    for c in range(8):
        b, j = c // 2, c % 2
        xm = np.ascontiguousarray(hs[b, j * R:(j + 1) * R].reshape(NT, 128, D))
        xp = zeros_half if j == 0 else np.ascontiguousarray(hs[b, 0:R].reshape(NT, 128, D))
        in_maps.append(dict(xm=xm, xp=xp, wqk=wqk, wv=wv, wg=wg, wo=wo, wp=wp, wn=wn,
                            nwb=nwb, ddt=ddt, qdkd=qdkd, idin=idin))
    return in_maps


def kernel(**inputs):
    if "nc" not in _CACHE:
        _CACHE["nc"] = _build_program()
    nc = _CACHE["nc"]
    in_maps = _host_prep(inputs)
    res = run_bass_kernel_spmd(nc, in_maps, core_ids=list(range(8)))
    out = np.empty((4, 2048, D), dtype=np.float32)
    for c in range(8):
        b, j = c // 2, c % 2
        out[b, j * R:(j + 1) * R] = np.asarray(res.results[c]["out"]).reshape(R, D)
    return out


# version 13
# speedup vs baseline: 1.0275x; 1.0275x over previous
import numpy as np
import contextlib
import concourse.bass as bass
import concourse.mybir as mybir
from concourse.bass_utils import run_bass_kernel_spmd

F32 = mybir.dt.float32
BF16 = mybir.dt.bfloat16
AF = mybir.ActivationFunctionType
ALU = mybir.AluOpType
AX = mybir.AxisListType

D = 4096
DC = 32
R = 1024
NT = 8
H = 32
HD = 128
BS = 256
NBLK = 4
EPS = 1e-5

A_OFF = 0
B_OFF = 32768
C_OFF = 65536
POOL_BF = C_OFF + 16896
WQK0 = C_OFF + 0
WQK1 = C_OFF + 2048
QB0 = C_OFF + 4096
QB1 = C_OFF + 5120
KF0 = C_OFF + 6144
KF1 = C_OFF + 7168
KT0 = C_OFF + 8192
KT1 = C_OFF + 9216
VT0 = C_OFF + 10240
DD0 = C_OFF + 12288
DD1 = C_OFF + 12800
SDT_O = C_OFF + 13312
KVB_O = C_OFF + 13824
IDT_O = C_OFF + 13952
TRH_O = C_OFF + 14080
X1T0 = C_OFF + 0
X1T1 = C_OFF + 4096
NW_O = C_OFF + 8192
WOCH0 = C_OFF + 0
WOCH1 = C_OFF + 8192
GST0 = C_OFF + 16384
GST1 = C_OFF + 16640
WD_O = C_OFF + 0
WPN0 = C_OFF + 8192
WPN1 = C_OFF + 10240
WPN2 = C_OFF + 12288
WPN3 = C_OFF + 14336

XF1 = 0
XF0H = 4096
KVA_O = 6144
QDKD_O = 10240
ST_O = 10368
POOL_F32 = 10432


def _decay_tables():
    try:
        import jax
        import jax.numpy as jnp
        with jax.default_device(jax.devices("cpu")[0]):
            base = 1.0 / 2.0 ** (8.0 / H)
            exponent = jnp.arange(H, dtype=jnp.float32) + 1.0
            factor = 1.0 - 1 / (H - 1 + 1e-05) + 1e-05
            slope = (base ** exponent * factor)[:, None, None]
            r = jnp.arange(BS, dtype=jnp.float32) + 1.0
            q_decay = jnp.exp(-slope * r[:, None])
            k_decay = jnp.exp(-slope * (BS - r)[:, None])
            diff = r[:, None] - r[None, :]
            diag = slope[None] * diff[None, None]
            diag = jnp.where(diag >= 0, -diag, -jnp.inf)
            diag_decay = jnp.exp(diag)
            block_decay = jnp.exp(-slope * BS)
            qd = np.asarray(q_decay)[:, :, 0].astype(np.float32)
            kd = np.asarray(k_decay)[:, :, 0].astype(np.float32)
            dd = np.asarray(diag_decay)[0].astype(np.float32)
            bd = np.asarray(block_decay)[:, 0, 0].astype(np.float64)
            return qd, kd, dd, [float(x) for x in bd]
    except Exception:
        pass
    base = np.float32(1.0 / 2.0 ** (8.0 / H))
    exponent = np.arange(H, dtype=np.float32) + np.float32(1.0)
    factor = np.float32(1.0 - 1 / (H - 1 + 1e-5) + 1e-5)
    slope = (base ** exponent) * factor
    r = np.arange(BS, dtype=np.float32) + 1.0
    qd = np.exp(-slope[:, None] * r[None, :]).astype(np.float32)
    kd = np.exp(-slope[:, None] * (BS - r)[None, :]).astype(np.float32)
    diff = r[:, None] - r[None, :]
    dmat = -slope[:, None, None] * diff[None, :, :]
    dmat = np.where(diff[None] >= 0, dmat, -np.inf)
    dd = np.exp(dmat.astype(np.float32)).astype(np.float32)
    bd = np.exp(-slope * np.float32(BS)).astype(np.float64)
    return qd, kd, dd, [float(x) for x in bd]


class _Sem:
    __slots__ = ("name", "h", "n", "eng", "step")

    def __init__(self, name, eng, step):
        self.name, self.h, self.n, self.eng, self.step = name, None, 0, eng, step


class _Ev:
    __slots__ = ("sem", "val")

    def __init__(self, sem, val):
        self.sem, self.val = sem, val


class _Prog:

    def __init__(self):
        self.ops = {"T": [], "V": [], "S": [], "Y": [], "G": []}
        self.sems = []
        self.base = {}

    def sem(self, name, eng, step=1):
        s = _Sem(name, eng, step)
        self.sems.append(s)
        if eng in ("V", "S") and eng not in self.base:
            self.base[eng] = s
        return s

    def emit(self, eng, fn, waits=(), inc=None):
        w = [ww for ww in waits if ww is not None]
        if eng in ("V", "S"):
            b = self.base[eng]
            if b.n > 0:
                w.append(_Ev(b, b.n))
            if inc is None:
                inc = b
        w = tuple((ww.sem, ww.val) for ww in w)
        self.ops[eng].append((w, fn, inc))
        if inc is not None:
            inc.n += inc.step
            ev = _Ev(inc, inc.n)
            self.last_ev = getattr(self, "last_ev", {})
            self.last_ev[eng] = ev
            return ev
        return None

    def mark(self, eng, sem):
        w, fn, inc = self.ops[eng][-1]
        if inc is None:
            sem.n += sem.step
            self.ops[eng][-1] = (w, fn, sem)
            ev = _Ev(sem, sem.n)
            self.last_ev = getattr(self, "last_ev", {})
            self.last_ev[eng] = ev
            return ev
        return self.last_ev[eng]


def _build_program(stop=99, stop2_hp=15, fill_tiles=NT, fill_ln=True, fill_tr=True,
                   och_ocs=16):
    qd_t, kd_t, dd_t, bd_t = _decay_tables()
    nc = bass.Bass("TRN2", target_bir_lowering=False)
    P = _Prog()
    _build_program.last_prog = P

    xm_d = nc.dram_tensor("xm", [NT, 128, D], F32, kind="ExternalInput").ap()
    xp_d = nc.dram_tensor("xp", [NT, 128, D], F32, kind="ExternalInput").ap()
    wqk_d = nc.dram_tensor("wqk", [128, 128, 2048], F32, kind="ExternalInput").ap()
    wv_d = nc.dram_tensor("wv", [64, 128, 2048], F32, kind="ExternalInput").ap()
    wg_d = nc.dram_tensor("wg", [64, 128, 2048], F32, kind="ExternalInput").ap()
    wo_d = nc.dram_tensor("wo", [64, 128, 2048], F32, kind="ExternalInput").ap()
    wp_d = nc.dram_tensor("wp", [64, 128, 2048], F32, kind="ExternalInput").ap()
    wn_d = nc.dram_tensor("wn", [64, 128, 2048], F32, kind="ExternalInput").ap()
    nwb_d = nc.dram_tensor("nwb", [128, D], F32, kind="ExternalInput").ap()
    ddt_d = nc.dram_tensor("ddt", [H, 128, 2 * BS], F32, kind="ExternalInput").ap()
    qdkd_d = nc.dram_tensor("qdkd", [128, 128], F32, kind="ExternalInput").ap()
    idin_d = nc.dram_tensor("idin", [128, 128], F32, kind="ExternalInput").ap()
    out_d = nc.dram_tensor("out", [NT, 128, D], F32, kind="ExternalOutput").ap()
    xres_d = nc.dram_tensor("xres", [NT, 128, D], F32, kind="Internal").ap()
    xre2_d = nc.dram_tensor("xre2", [NT, 128, D], F32, kind="Internal").ap()

    stack = contextlib.ExitStack()
    pb_h = stack.enter_context(nc.sbuf_tensor("pb", [128, POOL_BF], BF16))
    pf_h = stack.enter_context(nc.sbuf_tensor("pf", [128, POOL_F32], F32))
    PS = {}
    for k in "ABCDEF":
        PS[k] = stack.enter_context(nc.psum_tensor("ps" + k, [128, 512], F32))
    PT = [stack.enter_context(nc.psum_tensor("pt0", [128, 512], BF16)),
          stack.enter_context(nc.psum_tensor("pt1", [128, 512], BF16))]

    def bf(off, n):
        return pb_h[:, off:off + n]

    def f32(off, n):
        return pf_h[:, off:off + n]

    def stc(i):
        return pf_h[:, ST_O + i:ST_O + i + 1]

    IDT = bf(IDT_O, 128)
    A3 = pb_h[:, A_OFF:A_OFF + 32768].rearrange("p (c t) -> p c t", t=R)
    PT3 = [PT[i][:].rearrange("p (a t) -> p a t", t=128) for i in range(2)]

    sT = P.sem("sT", "T")
    sV = P.sem("sV", "V")
    sS = P.sem("sS", "S")
    yA = P.sem("yA", "Y", 16)
    yA1 = P.sem("yA1", "Y", 16)
    yB = P.sem("yB", "Y", 16)
    yB2 = P.sem("yB2", "Y", 16)
    yD = P.sem("yD", "Y", 16)
    yI = P.sem("yI", "Y", 16)
    gN = P.sem("gN", "G", 16)
    g0 = P.sem("g0", "G", 16)
    g1 = P.sem("g1", "G", 16)
    g2 = P.sem("g2", "G", 16)
    g3 = P.sem("g3", "G", 16)
    g4 = P.sem("g4", "G", 16)
    g5 = P.sem("g5", "G", 16)
    gI = P.sem("gI", "G", 16)

    bank_free = {}
    free_ev = {}

    def mm(ps_ap, lhsT, rhs, start, stop, bank, waits=(), inc=None):
        w = list(waits)
        if start:
            w.append(bank_free.get(bank))
        return P.emit("T", lambda e, o=ps_ap, l=lhsT, r=rhs, st=start, sp=stop:
                      e.matmul(o, l, r, start=st, stop=sp), waits=w, inc=inc)

    def mark(eng):
        s = {"T": sT, "V": sV, "S": sS}[eng]
        return P.mark(eng, s)

    ev_qdkd = P.emit("Y", lambda e: e.dma_start(out=f32(QDKD_O, 128), in_=qdkd_d), inc=yI)
    ev_ident = [P.emit("G", lambda e: e.dma_start(out=IDT, in_=idin_d), inc=gI)]

    def qd_ap(h, qc):
        return pf_h[:, QDKD_O + h * 2 + qc: QDKD_O + h * 2 + qc + 1]

    def kd_ap(h, kc):
        return pf_h[:, QDKD_O + 64 + h * 2 + kc: QDKD_O + 64 + h * 2 + kc + 1]

    stats_busy = [None]

    def rstd_refine(col_var, col_out, eps, waits):
        evv = P.emit("V", lambda e: e.tensor_scalar_add(stc(8), col_var, float(eps)),
                     waits=waits, inc=sV)
        evs = P.emit("S", lambda e: e.activation(stc(9), stc(8), AF.Sqrt), waits=[evv], inc=sS)
        P.emit("V", lambda e: e.reciprocal(stc(10), stc(9)), waits=[evs])
        for _ in range(1):
            P.emit("V", lambda e: e.tensor_mul(stc(11), stc(10), stc(10)))
            P.emit("V", lambda e: e.tensor_mul(stc(11), stc(11), stc(8)))
            P.emit("V", lambda e: e.tensor_scalar(stc(11), stc(11), -0.5, 1.5, ALU.mult, ALU.add))
            P.emit("V", lambda e: e.tensor_mul(stc(10), stc(11), stc(10)))
        return P.emit("V", lambda e, co=col_out: e.tensor_copy(co, stc(10)), inc=sV)

    def row_sumsq(src_ap, out_col, waits):
        n = src_ap.shape[-1]
        n2 = n // 2
        e1 = P.emit("S", lambda e, s=src_ap: e.activation(bf(TRH_O, n2), s[:, 0:n2], AF.Square,
                                                          accum_out=stc(12)), waits=waits, inc=sS)
        e2 = P.emit("S", lambda e, s=src_ap: e.activation(bf(TRH_O, n2), s[:, n2:n], AF.Square,
                                                          accum_out=stc(13)), waits=waits, inc=sS)
        return P.emit("V", lambda e, oc=out_col: e.tensor_add(oc, stc(12), stc(13)),
                      waits=[e1, e2], inc=sV)

    def row_stats(src_ap, mean_col, var_col, waits):
        n = src_ap.shape[-1]
        ev_s = P.emit("V", lambda e, s=src_ap: e.reduce_sum(stc(14), s, axis=AX.X),
                      waits=waits, inc=sV)
        ev_q = row_sumsq(src_ap, stc(15), waits)
        P.emit("V", lambda e, mc=mean_col: e.tensor_scalar_mul(mc, stc(14), 1.0 / n), waits=[ev_s])
        P.emit("V", lambda e: e.tensor_scalar_mul(stc(15), stc(15), 1.0 / n), waits=[ev_q])
        P.emit("V", lambda e, mc=mean_col: e.tensor_mul(stc(14), mc, mc))
        return P.emit("V", lambda e, vc=var_col: e.tensor_sub(vc, stc(15), stc(14)), inc=sV)

    def transpose_tile_to_A(src_off, tt, src_ready, extra_first_waits=()):
        last = None
        for cg in range(8):
            pi = cg % 2
            evt = None
            for ci in range(4):
                c = cg * 4 + ci
                w = [src_ready, ev_ident[-1]]
                if ci == 0:
                    w.append(bank_free.get(("pt", pi)))
                w += list(extra_first_waits) if (cg == 0 and ci == 0) else []
                evt = P.emit("T", lambda e, cc=c, b=pi, ii=ci, so=src_off: e.transpose(
                    PT[b][:, ii * 128:(ii + 1) * 128],
                    bf(so + cc * 128, 128), IDT),
                    waits=w, inc=sT if ci == 3 else None)
            ev_ev = P.emit("S", lambda e, cg_=cg, t=tt, b=pi: e.activation(
                A3[:, cg_ * 4:cg_ * 4 + 4, t * 128:(t + 1) * 128],
                PT3[b][:, 0:4, :], AF.Copy),
                waits=[evt], inc=sS)
            bank_free[("pt", pi)] = ev_ev
            last = ev_ev
        return last

    def emit_fill_A(src_d, a_readers_done, slots=(XF1,)):
        last = None
        for tt in range(fill_tiles):
            sl = tt % 2
            xfo = slots[tt % len(slots)]
            ev_ld = P.emit("Y", lambda e, t=tt, o=xfo: e.dma_start(
                out=f32(o, 4096), in_=src_d[t]),
                waits=[free_ev.get(("xf", xfo))], inc=(yA if xfo == XF1 else yA1))
            if not fill_ln:
                last = ev_ld
                free_ev[("xf", xfo)] = ev_ld
                continue
            xf = f32(xfo, 4096)
            ev_var = row_stats(xf, stc(0), stc(1), [ev_ld, stats_busy[0]])
            ev_rstd = rstd_refine(stc(1), stc(2), EPS, [ev_var])
            ev_ap = P.emit("V", lambda e, s=sl, o=xfo: e.tensor_scalar(
                bf(X1T0 + s * 4096, 4096), f32(o, 4096), stc(0), stc(2),
                ALU.subtract, ALU.mult),
                waits=[free_ev.get(("x1t", sl)), ev_rstd], inc=sV)
            stats_busy[0] = ev_ap
            free_ev[("xf", xfo)] = ev_ap
            if not fill_tr:
                last = ev_ap
                free_ev[("x1t", sl)] = ev_ap
                continue
            last = transpose_tile_to_A(X1T0 + sl * 4096, tt, ev_ap,
                                       extra_first_waits=[a_readers_done])
            free_ev[("x1t", sl)] = last
        return last

    wslot_i = [0]
    wsem = [g0, g1]

    def load_wslot(src_ap):
        i = wslot_i[0] % 2
        wslot_i[0] += 1
        ev = P.emit("G", lambda e, ii=i, s=src_ap: e.dma_start(
            out=bf(WQK0 + ii * 2048, 2048), in_=s),
            waits=[free_ev.get(("wslot", i))], inc=wsem[i])
        return i, ev

    def proj_fm(g, banks, a_ready):
        fills = [None, None]
        for half in range(2):
            sl, ev_ld = load_wslot(wqk_d[2 * g + half])
            last = None
            for dkl in range(16):
                dk = half * 16 + dkl
                for th in range(2):
                    last = mm(PS[banks[th]][:, 0:512],
                              bf(WQK0 + sl * 2048 + dkl * 128, 128),
                              bf(A_OFF + dk * R + th * 512, 512),
                              start=(dk == 0), stop=(dk == 31), bank=banks[th],
                              waits=[ev_ld, a_ready],
                              inc=sT if dkl == 15 else None)
                    if dk == 31:
                        fills[th] = last
            free_ev[("wslot", sl)] = last
        return fills

    def emit_headpair(hp, prefix, a_ready, kv_init_ev):
        for hh in range(2):
            h = 2 * hp + hh
            hs = h % 2
            qb = QB0 if hs == 0 else QB1
            kf = KF0 if hs == 0 else KF1
            kt = KT0 if hs == 0 else KT1
            ddo = DD0 if hs == 0 else DD1
            if not prefix:
                ev_dd = P.emit("G", lambda e, h_=h, d_=ddo: e.dma_start(
                    out=bf(d_, 512), in_=ddt_d[h_]),
                    waits=[free_ev.get(("dd", hs))], inc=(g2 if hs == 0 else g3))
                free_ev[("ddld", h)] = ev_dd
                fills = proj_fm(h, ("A", "B"), a_ready)
                for th in range(2):
                    ev = P.emit("S", lambda e, t=th, q=qb, b=("A", "B")[th]: e.activation(
                        bf(q + t * 512, 512), PS[b][:, 0:512], AF.Silu),
                        waits=[fills[th], free_ev.get(("qb", hs))], inc=sS)
                    bank_free[("A", "B")[th]] = ev
                    free_ev[("qrdy", h, th)] = ev
            fills = proj_fm(32 + h, ("C", "D"), a_ready)
            for th in range(2):
                ev = P.emit("S", lambda e, t=th, k=kf, b=("C", "D")[th]: e.activation(
                    bf(k + t * 512, 512), PS[b][:, 0:512], AF.Silu),
                    waits=[fills[th], free_ev.get(("kf", hs))], inc=sS)
                bank_free[("C", "D")[th]] = ev
                free_ev[("krdy", h, th)] = ev
            for tt in range(NT):
                pi = tt % 2
                w = [free_ev[("krdy", h, tt // 4)], ev_ident[-1], bank_free.get(("pt", pi))]
                if tt < 2:
                    w.append(free_ev.get(("kt", hs)))
                evt = P.emit("T", lambda e, k=kf, t=tt, b=pi: e.transpose(
                    PT[b][:, 0:128],
                    bf(k + t * 128, 128), IDT), waits=w, inc=sT)
                ev_ev = P.emit("S", lambda e, t=tt, k2=kt, b=pi, h_=h: e.activation(
                    bf(k2 + t * 128, 128),
                    PT[b][:, 0:128], AF.Copy,
                    scale=kd_ap(h_, t % 2)),
                    waits=[evt, ev_qdkd], inc=sS)
                bank_free[("pt", pi)] = ev_ev
                free_ev[("ktm", h, tt)] = ev_ev

        vto = VT0
        for p2 in range(2):
            vlast = None
            for sl4 in range(4):
                sl, ev_ld = load_wslot(wv_d[hp * 4 + sl4])
                for a in range(8):
                    dk = sl4 * 8 + a
                    for t4 in range(4):
                        tt = p2 * 4 + t4
                        bk = "ABCD"[t4]
                        vlast = mm(PS[bk][:, 0:256],
                                   bf(A_OFF + dk * R + tt * 128, 128),
                                   bf(WQK0 + sl * 2048 + a * 256, 256),
                                   start=(dk == 0), stop=(dk == 31), bank=bk,
                                   waits=[ev_ld, a_ready],
                                   inc=sT if (a == 7 and t4 == 3) else None)
                free_ev[("wslot", sl)] = vlast
            for t4 in range(4):
                tt = p2 * 4 + t4
                bk = "ABCD"[t4]
                ev = P.emit("S", lambda e, t=tt, b=bk, v=vto: e.activation(
                    bf(v + t * 256, 256), PS[b][:, 0:256], AF.Silu),
                    waits=[vlast, free_ev.get(("vt",))], inc=sS)
                free_ev[("vrdy", hp, tt)] = ev
                bank_free[bk] = ev

        for hh in range(2):
            h = 2 * hp + hh
            hs = h % 2
            qb = QB0 if hs == 0 else QB1
            kf = KF0 if hs == 0 else KF1
            kt = KT0 if hs == 0 else KT1
            ddo = DD0 if hs == 0 else DD1
            kvo = KVA_O + hp * 256 + hh * 128
            kv = f32(kvo, 128)
            last_sdt = [None, None]
            last_inter = None
            last_state = None
            for n in range(NBLK):
                if not prefix:
                    w = [kv_init_ev, free_ev.get("kvb")]
                    ev_kvb = P.emit("V", lambda e, k=kv: e.tensor_copy(bf(KVB_O, 128), k),
                                    waits=w, inc=sV)
                    ev_sd = [None, None]
                    for ktc in range(2):
                        tile = 2 * n + ktc
                        bk = "EF"[ktc]
                        evs_ = mm(PS[bk][:, 0:256], bf(kf + tile * 128, 128),
                                  bf(qb + n * 256, 256), start=True, stop=True, bank=bk,
                                  waits=[free_ev[("krdy", h, tile // 4)],
                                         free_ev[("qrdy", h, n // 2)]],
                                  inc=sT)
                        ev_sd[ktc] = P.emit("V", lambda e, k2=ktc, b=bk, h_=h: e.tensor_tensor(
                            bf(SDT_O + k2 * 256, 256), PS[b][:, 0:256],
                            bf((DD0 if h_ % 2 == 0 else DD1) + k2 * 256, 256), ALU.mult),
                            waits=[evs_, free_ev[("ddld", h)], free_ev.get("sdt")], inc=sV)
                        bank_free[bk] = ev_sd[ktc]
                    ev_emain = [None, None]
                    for qtc in range(2):
                        tt = 2 * n + qtc
                        bmain = "CD"[qtc]
                        binter = "AB"[qtc]
                        evm = None
                        for ktc in range(2):
                            evm = mm(PS[bmain][:, 0:128],
                                     bf(SDT_O + ktc * 256 + qtc * 128, 128),
                                     bf(vto + (2 * n + ktc) * 256 + hh * 128, 128),
                                     start=(ktc == 0), stop=(ktc == 1), bank=bmain,
                                     waits=[ev_sd[ktc], free_ev[("vrdy", hp, 2 * n + ktc)]],
                                     inc=sT if ktc == 1 else None)
                        evi = mm(PS[binter][:, 0:128], bf(qb + tt * 128, 128),
                                 bf(KVB_O, 128), start=True, stop=True, bank=binter,
                                 waits=[ev_kvb, free_ev[("qrdy", h, n // 2)]], inc=sT)
                        boff = B_OFF + tt * 4096 + h * 128
                        P.emit("V", lambda e, bo=boff, b=binter, h_=h, q=qtc: e.tensor_scalar_mul(
                            bf(bo, 128), PS[b][:, 0:128], qd_ap(h_, q)),
                            waits=[evi, ev_qdkd])
                        eva = P.emit("V", lambda e, bo=boff, b=bmain: e.tensor_add(
                            bf(bo, 128), bf(bo, 128), PS[b][:, 0:128]),
                            waits=[evm], inc=sV)
                        bank_free[bmain] = eva
                        bank_free[binter] = eva
                        ev_emain[qtc] = eva
                        last_inter = eva
                    free_ev["sdt"] = ev_emain[1]
                    free_ev["kvb"] = _last_T_inter = None
                    free_ev["kvb"] = ev_emain[1]
                evst = None
                for ktc in range(2):
                    tile = 2 * n + ktc
                    evst = mm(PS["F"][:, 128:256], bf(kt + tile * 128, 128),
                              bf(vto + tile * 256 + hh * 128, 128),
                              start=(ktc == 0), stop=(ktc == 1), bank="F",
                              waits=[free_ev[("ktm", h, tile)], free_ev[("vrdy", hp, tile)]],
                              inc=sT if ktc == 1 else None)
                w = [evst, kv_init_ev]
                ev_upd = P.emit("V", lambda e, k=kv, b=bd_t[h]: e.scalar_tensor_tensor(
                    k, k, b, PS["F"][:, 128:256], ALU.mult, ALU.add), waits=w, inc=sV)
                bank_free["F"] = ev_upd
                last_state = ev_upd
            free_ev[("qb", hs)] = last_inter if not prefix else None
            free_ev[("kf", hs)] = last_inter if not prefix else last_state
            free_ev[("kt", hs)] = last_state
            free_ev[("dd", hs)] = last_inter if not prefix else None
            free_ev[("kvdone", h)] = last_state
        free_ev[("vt",)] = free_ev[("kvdone", 2 * hp + 1)]

    def emit_och_matmul(w_dram, a_ready, evac_fn, first_w=(), first_w_odd=None,
                        name=""):
        grp = [("A", "B", "C"), ("D", "E", "F")]
        last_evs = []
        for oc in range(och_ocs):
            wbuf = WOCH0 if oc % 2 == 0 else WOCH1
            gsem = g2 if oc % 2 == 0 else g3
            fw = list(first_w) if (oc % 2 == 0 or first_w_odd is None) else list(first_w_odd)
            evw = None
            for piece in range(4):
                evw = P.emit("G", lambda e, s=w_dram[oc * 4 + piece], wb=wbuf, pp=piece: e.dma_start(
                    out=bf(wb + pp * 2048, 2048), in_=s),
                    waits=[free_ev.get(("woch", oc % 2))] + (fw if piece == 0 else []),
                    inc=gsem)
            banks = grp[oc % 2]
            lastmm = None
            for tt in range(NT):
                bk = banks[tt % 3]
                for dk in range(DC):
                    lastmm = mm(PS[bk][:, 0:256],
                                bf(A_OFF + dk * R + tt * 128, 128),
                                bf(wbuf + (dk // 8) * 2048 + (dk % 8) * 256, 256),
                                start=(dk == 0), stop=(dk == DC - 1),
                                bank=(bk if dk == 0 else None),
                                waits=[evw, a_ready],
                                inc=sT if dk == DC - 1 else None)
                ev = evac_fn(oc, tt, PS[bk][:, 0:256], [lastmm])
                bank_free[bk] = ev
                last_evs.append(ev)
            free_ev[("woch", oc % 2)] = lastmm
        return last_evs[-1]

    def _materialize():
        for s in P.sems:
            s.h = stack.enter_context(nc.semaphore(s.name))

        with nc.Block() as block:
            def runner(engname):
                def run(e):
                    hw = {}
                    for waits, fn, inc in P.ops[engname]:
                        for sem, val in waits:
                            if hw.get(sem.name, 0) < val:
                                e.wait_ge(sem.h, val)
                                hw[sem.name] = val
                        inst = fn(e)
                        if inc is not None:
                            inst.then_inc(inc.h, inc.step)
                return run

            block.tensor(runner("T"))
            block.vector(runner("V"))
            block.scalar(runner("S"))
            block.sync(runner("Y"))
            block.gpsimd(runner("G"))

        stack.close()
        return nc

    def _early_out(last_ev):
        for tt in range(NT):
            P.emit("Y", lambda e, t=tt: e.dma_start(out=out_d[t], in_=f32(XF0, 4096)),
                   waits=[last_ev], inc=yD)
        P.emit("G", lambda e: e.wait_ge(yD.h, yD.n), waits=())
        return _materialize()

    ev_a1 = emit_fill_A(xp_d, None, slots=(XF1, KVA_O))
    if stop <= 1:
        return _early_out(ev_a1)
    free_ev[("wslot", 0)] = ev_a1
    free_ev[("wslot", 1)] = ev_a1
    prefix_done = {}
    for hp in range(16):
        ev_init = P.emit("V", lambda e, o=KVA_O + hp * 256: e.memset(f32(o, 256), 0.0),
                         inc=sV)
        emit_headpair(hp, prefix=True, a_ready=ev_a1, kv_init_ev=ev_init)
        prefix_done[hp] = free_ev[("kvdone", 2 * hp + 1)]
        if stop <= 2 and hp >= stop2_hp:
            break
    ev_prefix_T = mark("T")
    if stop <= 2:
        return _early_out(mark("V"))

    ev_a2 = emit_fill_A(xm_d, ev_prefix_T)
    if stop <= 3:
        return _early_out(ev_a2)
    free_ev[("wslot", 0)] = ev_a2
    free_ev[("wslot", 1)] = ev_a2

    for hp in range(16):
        emit_headpair(hp, prefix=False, a_ready=ev_a2, kv_init_ev=prefix_done[hp])
        if stop <= 4 and hp >= stop2_hp:
            break
    ev_attn_T = mark("T")
    ev_attn_V = mark("V")
    if stop <= 4:
        return _early_out(ev_attn_V)

    ev_nw = P.emit("G", lambda e: e.dma_start(out=bf(NW_O, 4096), in_=nwb_d),
                   waits=[ev_attn_T], inc=gN)
    ev_norm_last = None
    for tt in range(NT):
        bt = bf(B_OFF + tt * 4096, 4096)
        ev_q = row_sumsq(bt, stc(0), [ev_attn_V, stats_busy[0]])
        P.emit("V", lambda e: e.tensor_scalar_mul(stc(0), stc(0), 1.0 / D), waits=[ev_q])
        ev_r = rstd_refine(stc(0), stc(1), EPS, [])
        ev1 = P.emit("V", lambda e, b=bt: e.tensor_scalar_mul(b, b, stc(1)), waits=[ev_r], inc=sV)
        stats_busy[0] = ev1
        ev_norm_last = P.emit("V", lambda e, b=bt: e.tensor_mul(b, b, bf(NW_O, 4096)),
                              waits=[ev_nw], inc=sV)
    if stop <= 5:
        return _early_out(ev_norm_last)

    def gate_evac(oc, tt, ps_ap, waits):
        gsl = GST0 if tt % 2 == 0 else GST1
        evs_ = P.emit("S", lambda e, g=gsl, p=ps_ap: e.activation(bf(g, 256), p, AF.Sigmoid),
                      waits=waits + [free_ev.get(("gst", tt % 2)), ev_norm_last], inc=sS)
        bo = B_OFF + tt * 4096 + oc * 256
        ev = P.emit("V", lambda e, b=bo, g=gsl: e.tensor_mul(bf(b, 256), bf(b, 256), bf(g, 256)),
                    waits=[evs_], inc=sV)
        free_ev[("gst", tt % 2)] = ev
        return ev

    ev_gate_last = emit_och_matmul(wg_d, ev_a2, gate_evac,
                                   first_w=[ev_attn_T],
                                   first_w_odd=[ev_attn_T, ev_norm_last], name="gate")
    ev_gate_T = mark("T")
    if stop <= 6:
        return _early_out(ev_gate_last)

    ev_ident.append(P.emit("G", lambda e: e.dma_start(out=IDT, in_=idin_d),
                           waits=[ev_gate_T], inc=gI))
    ev_ng = None
    for tt in range(NT):
        ev_ng = transpose_tile_to_A(B_OFF + tt * 4096, tt, ev_gate_last,
                                    extra_first_waits=[ev_gate_T])
    ev_ng_T = mark("T")
    if stop <= 7:
        return _early_out(ev_ng)

    def y_evac(oc, tt, ps_ap, waits):
        bo = B_OFF + tt * 4096 + oc * 256
        return P.emit("S", lambda e, b=bo, p=ps_ap: e.activation(bf(b, 256), p, AF.Copy),
                      waits=waits + [ev_ng_T if oc == 0 else None], inc=sS)

    ev_y_last = emit_och_matmul(wo_d, ev_ng, y_evac, first_w=[ev_ng_T], name="out")
    ev_out_T = mark("T")
    if stop <= 8:
        return _early_out(ev_y_last)

    TMP = f32(KVA_O, 4096)
    for tt in range(NT):
        bt = bf(B_OFF + tt * 4096, 4096)
        ev_var = row_stats(bt, stc(0), stc(1), [ev_y_last, stats_busy[0]])
        ev_r = rstd_refine(stc(1), stc(2), EPS, [ev_var])
        ev_t = P.emit("V", lambda e, b=bt: e.tensor_scalar(
            f32(XF1, 4096), b, stc(0), stc(2), ALU.subtract, ALU.mult),
            waits=[ev_r, free_ev.get(("xf", XF1))], inc=sV)
        ev_hi = None
        for h2 in range(2):
            off = h2 * 2048
            ev_ldx = P.emit("Y", lambda e, t=tt, o=off: e.dma_start(
                out=f32(XF0H, 2048), in_=xm_d[t][:, o:o + 2048]),
                waits=[free_ev.get(("xf8h",)) or free_ev.get(("xf", XF1))], inc=yA)
            ev_hi = P.emit("V", lambda e, o=off: e.tensor_add(
                pf_h[:, KVA_O + o:KVA_O + o + 2048], f32(XF1 + o, 2048), f32(XF0H, 2048)),
                waits=[ev_ldx, ev_t, free_ev.get(("xfhi",)) if h2 == 0 else None], inc=sV)
            P.emit("V", lambda e, o=off: e.tensor_sub(
                f32(XF0H, 2048), f32(XF0H, 2048), pf_h[:, KVA_O + o:KVA_O + o + 2048]),
                inc=sV)
            ev_lo = P.emit("V", lambda e, o=off: e.tensor_add(
                f32(XF0H, 2048), f32(XF0H, 2048), f32(XF1 + o, 2048)), inc=sV)
            ev_sl = P.emit("Y", lambda e, t=tt, o=off: e.dma_start(
                out=xre2_d[t][:, o:o + 2048], in_=f32(XF0H, 2048)),
                waits=[ev_lo], inc=yB2)
            free_ev[("xf8h",)] = ev_sl
        free_ev[("xre2", tt)] = free_ev[("xf8h",)]
        ev_sh = P.emit("Y", lambda e, t=tt: e.dma_start(out=xres_d[t], in_=TMP),
                       waits=[ev_hi], inc=yB)
        free_ev[("xres", tt)] = ev_sh
        free_ev[("xfhi",)] = ev_sh
        ev_var2 = row_stats(TMP, stc(3), stc(4), [ev_hi])
        ev_r2 = rstd_refine(stc(4), stc(5), EPS, [ev_var2])
        ev_x2 = P.emit("V", lambda e, b=bt: e.tensor_scalar(
            b, TMP, stc(3), stc(5), ALU.subtract, ALU.mult),
            waits=[ev_r2], inc=sV)
        stats_busy[0] = ev_x2
        free_ev[("x2", tt)] = ev_x2
    ev_x2_all = mark("V")
    if stop <= 9:
        return _early_out(ev_x2_all)

    ev_ident.append(P.emit("G", lambda e: e.dma_start(out=IDT, in_=idin_d),
                           waits=[ev_out_T], inc=gI))
    ev_x2fm = None
    for tt in range(NT):
        ev_x2fm = transpose_tile_to_A(B_OFF + tt * 4096, tt, free_ev[("x2", tt)],
                                      extra_first_waits=[ev_out_T])
    ev_x2fm_T = mark("T")
    if stop <= 10:
        return _early_out(ev_x2fm)

    wpn_sem = {0: g2, 1: g3, 2: g4, 3: g5}
    wd_ready = {}
    for oc in range(16):
        for sl4 in range(4):
            sp = WPN0 if sl4 % 2 == 0 else WPN1
            sn = WPN2 if sl4 % 2 == 0 else WPN3
            idx = sl4 % 2
            evp = P.emit("G", lambda e, s=sp, src=wp_d[oc * 4 + sl4]: e.dma_start(
                out=bf(s, 2048), in_=src),
                waits=[free_ev.get(("wpn", idx)), ev_x2fm_T if (oc == 0 and sl4 == 0) else None],
                inc=wpn_sem[idx])
            evn = P.emit("G", lambda e, s=sn, src=wn_d[oc * 4 + sl4]: e.dma_start(
                out=bf(s, 2048), in_=src),
                waits=[free_ev.get(("wpn2", idx))], inc=wpn_sem[idx + 2])
            ev1 = P.emit("V", lambda e, s=sp, q=sl4: e.tensor_scalar(
                bf(WD_O + q * 2048, 2048), bf(s, 2048), 0.0, None, ALU.is_gt),
                waits=[evp, free_ev.get(("wd", sl4))], inc=sV)
            free_ev[("wpn", idx)] = ev1
            ev2 = P.emit("V", lambda e, s=sn: e.tensor_scalar(
                bf(s, 2048), bf(s, 2048), 0.0, None, ALU.is_gt),
                waits=[evn], inc=sV)
            ev3 = P.emit("V", lambda e, s=sn, q=sl4: e.tensor_sub(
                bf(WD_O + q * 2048, 2048), bf(WD_O + q * 2048, 2048), bf(s, 2048)),
                waits=[ev1, ev2], inc=sV)
            free_ev[("wpn2", idx)] = ev3
            wd_ready[sl4] = ev3
        banks = [("A", "B", "C"), ("D", "E", "F")][oc % 2]
        lastmm = None
        for tt in range(NT):
            bk = banks[tt % 3]
            for dk in range(DC):
                lastmm = mm(PS[bk][:, 0:256],
                            bf(A_OFF + dk * R + tt * 128, 128),
                            bf(WD_O + (dk // 8) * 2048 + (dk % 8) * 256, 256),
                            start=(dk == 0), stop=(dk == DC - 1),
                            bank=(bk if dk == 0 else None),
                            waits=[wd_ready[dk // 8], ev_x2fm if (dk == 0 and tt == 0) else None],
                            inc=sT if dk == DC - 1 else None)
            bo = B_OFF + tt * 4096 + oc * 256
            ev = P.emit("S", lambda e, b=bo, ps=PS[bk]: e.activation(
                bf(b, 256), ps[:, 0:256], AF.Copy),
                waits=[lastmm, ev_x2_all if oc == 0 else None], inc=sS)
            bank_free[bk] = ev
        for q in range(4):
            free_ev[("wd", q)] = lastmm
    ev_ffn_T = mark("T")
    ev_ffn_V = mark("V")
    if stop <= 11:
        return _early_out(ev_ffn_V)

    ph8_x2 = stats_busy[0]
    for tt in range(NT):
        bt = bf(B_OFF + tt * 4096, 4096)
        ev_var = row_stats(bt, stc(0), stc(1), [ev_ffn_V, ev_ffn_T, stats_busy[0]])
        ev_r = rstd_refine(stc(1), stc(2), EPS, [ev_var])
        ev_t = P.emit("V", lambda e, b=bt: e.tensor_scalar(
            f32(XF1, 4096), b, stc(0), stc(2), ALU.subtract, ALU.mult),
            waits=[ev_r, free_ev.get(("xf11", 1))], inc=sV)
        w1 = [free_ev[("xre2", tt)], free_ev.get(("xf11t",))]
        if tt == 0:
            w1 += [free_ev[("xres", NT - 1)], ph8_x2]
        ev_ld2 = P.emit("Y", lambda e, t=tt: e.dma_start(out=TMP, in_=xre2_d[t]),
                        waits=w1, inc=yI)
        ev_o2 = P.emit("V", lambda e: e.tensor_add(f32(XF1, 4096), f32(XF1, 4096), TMP),
                       waits=[ev_ld2, ev_t], inc=sV)
        free_ev[("xf11t",)] = ev_o2
        ev_o = None
        for h2 in range(2):
            off = h2 * 2048
            ev_ldh = P.emit("Y", lambda e, t=tt, o=off: e.dma_start(
                out=f32(XF0H, 2048), in_=xres_d[t][:, o:o + 2048]),
                waits=[free_ev[("xres", tt)], free_ev.get(("xf8h",))], inc=yA)
            ev_o = P.emit("V", lambda e, o=off: e.tensor_add(
                f32(XF1 + o, 2048), f32(XF1 + o, 2048), f32(XF0H, 2048)),
                waits=[ev_ldh, ev_o2], inc=sV)
            free_ev[("xf8h",)] = ev_o
        ev_st = P.emit("Y", lambda e, t=tt: e.dma_start(out=out_d[t], in_=f32(XF1, 4096)),
                       waits=[ev_o], inc=yD)
        free_ev[("xf11", 1)] = ev_st

    P.emit("G", lambda e: e.wait_ge(yD.h, yD.n), waits=())
    return _materialize()


_CACHE = {}


def _pack_och(wT):
    return np.ascontiguousarray(
        wT.reshape(4, 8, 128, 16, 256).transpose(3, 0, 2, 1, 4).reshape(64, 128, 2048))


def _host_prep(inputs):
    hs = np.asarray(inputs["L0_attn_hidden_states"], dtype=np.float32)
    qkv_w = np.asarray(inputs["L0_attn_qkv_w"], dtype=np.float32)
    out_w = np.asarray(inputs["L0_attn_out_w"], dtype=np.float32)
    gate_w = np.asarray(inputs["L0_attn_gate_w"], dtype=np.float32)
    norm_w = np.asarray(inputs["L0_attn_norm_w"], dtype=np.float32)
    w_pos = np.asarray(inputs["L0_ffn_w_pos"], dtype=np.float32)
    w_neg = np.asarray(inputs["L0_ffn_w_neg"], dtype=np.float32)

    wq = np.ascontiguousarray(
        qkv_w.reshape(H, 3, HD, D).transpose(1, 0, 2, 3).reshape(3 * D, D).T)
    wqk = np.ascontiguousarray(
        wq[:, :2 * D].reshape(2, 16, 128, 64, 128).transpose(3, 0, 2, 1, 4).reshape(
            128, 128, 2048))
    wv = _pack_och(wq[:, 2 * D:])
    wg = _pack_och(np.ascontiguousarray(gate_w.T))
    wo = _pack_och(np.ascontiguousarray(out_w.T))
    wp = _pack_och(np.ascontiguousarray(w_pos.T))
    wn = _pack_och(np.ascontiguousarray(w_neg.T))
    nwb = np.ascontiguousarray(np.broadcast_to(norm_w[None, :], (128, D)))

    qd_t, kd_t, dd_t, _bd = _decay_tables()
    ddt = np.ascontiguousarray(
        dd_t.reshape(H, BS, 2, 128).transpose(0, 3, 2, 1).reshape(H, 128, 2 * BS))
    qdkd = np.zeros((128, 128), dtype=np.float32)
    for h in range(H):
        for c in range(2):
            qdkd[:, h * 2 + c] = qd_t[h, c * 128:(c + 1) * 128]
            qdkd[:, 64 + h * 2 + c] = kd_t[h, c * 128:(c + 1) * 128]
    idin = np.eye(128, dtype=np.float32)

    zeros_half = np.zeros((NT, 128, D), dtype=np.float32)
    in_maps = []
    for c in range(8):
        b, j = c // 2, c % 2
        xm = np.ascontiguousarray(hs[b, j * R:(j + 1) * R].reshape(NT, 128, D))
        xp = zeros_half if j == 0 else np.ascontiguousarray(hs[b, 0:R].reshape(NT, 128, D))
        in_maps.append(dict(xm=xm, xp=xp, wqk=wqk, wv=wv, wg=wg, wo=wo, wp=wp, wn=wn,
                            nwb=nwb, ddt=ddt, qdkd=qdkd, idin=idin))
    return in_maps


def kernel(**inputs):
    if "nc" not in _CACHE:
        _CACHE["nc"] = _build_program()
    nc = _CACHE["nc"]
    in_maps = _host_prep(inputs)
    res = run_bass_kernel_spmd(nc, in_maps, core_ids=list(range(8)))
    out = np.empty((4, 2048, D), dtype=np.float32)
    for c in range(8):
        b, j = c // 2, c % 2
        out[b, j * R:(j + 1) * R] = np.asarray(res.results[c]["out"]).reshape(R, D)
    return out


# version 14 (final)
# speedup vs baseline: 1.0545x; 1.0263x over previous
import numpy as np
import contextlib
import concourse.bass as bass
import concourse.mybir as mybir
from concourse.bass_utils import run_bass_kernel_spmd

F32 = mybir.dt.float32
BF16 = mybir.dt.bfloat16
AF = mybir.ActivationFunctionType
ALU = mybir.AluOpType
AX = mybir.AxisListType

D = 4096
DC = 32
R = 1024
NT = 8
H = 32
HD = 128
BS = 256
NBLK = 4
EPS = 1e-5

A_OFF = 0
B_OFF = 32768
C_OFF = 65536
POOL_BF = C_OFF + 16896
WQK0 = C_OFF + 0
WQK1 = C_OFF + 2048
QB0 = C_OFF + 4096
QB1 = C_OFF + 5120
KF0 = C_OFF + 6144
KF1 = C_OFF + 7168
KT0 = C_OFF + 8192
KT1 = C_OFF + 9216
VT0 = C_OFF + 10240
DD0 = C_OFF + 12288
DD1 = C_OFF + 12800
SDT_O = C_OFF + 13312
KVB_O = C_OFF + 13824
IDT_O = C_OFF + 13952
TRH_O = C_OFF + 14080
X1T0 = C_OFF + 0
X1T1 = C_OFF + 4096
NW_O = C_OFF + 8192
WOCH0 = C_OFF + 0
WOCH1 = C_OFF + 8192
GST0 = C_OFF + 16384
GST1 = C_OFF + 16640
WD_O = C_OFF + 0
WPN0 = C_OFF + 8192
WPN1 = C_OFF + 10240
WPN2 = C_OFF + 12288
WPN3 = C_OFF + 14336

XF1 = 0
XF0H = 4096
KVA_O = 6144
QDKD_O = 10240
ST_O = 10368
POOL_F32 = 10432


def _decay_tables():
    try:
        import jax
        import jax.numpy as jnp
        with jax.default_device(jax.devices("cpu")[0]):
            base = 1.0 / 2.0 ** (8.0 / H)
            exponent = jnp.arange(H, dtype=jnp.float32) + 1.0
            factor = 1.0 - 1 / (H - 1 + 1e-05) + 1e-05
            slope = (base ** exponent * factor)[:, None, None]
            r = jnp.arange(BS, dtype=jnp.float32) + 1.0
            q_decay = jnp.exp(-slope * r[:, None])
            k_decay = jnp.exp(-slope * (BS - r)[:, None])
            diff = r[:, None] - r[None, :]
            diag = slope[None] * diff[None, None]
            diag = jnp.where(diag >= 0, -diag, -jnp.inf)
            diag_decay = jnp.exp(diag)
            block_decay = jnp.exp(-slope * BS)
            qd = np.asarray(q_decay)[:, :, 0].astype(np.float32)
            kd = np.asarray(k_decay)[:, :, 0].astype(np.float32)
            dd = np.asarray(diag_decay)[0].astype(np.float32)
            bd = np.asarray(block_decay)[:, 0, 0].astype(np.float64)
            return qd, kd, dd, [float(x) for x in bd]
    except Exception:
        pass
    base = np.float32(1.0 / 2.0 ** (8.0 / H))
    exponent = np.arange(H, dtype=np.float32) + np.float32(1.0)
    factor = np.float32(1.0 - 1 / (H - 1 + 1e-5) + 1e-5)
    slope = (base ** exponent) * factor
    r = np.arange(BS, dtype=np.float32) + 1.0
    qd = np.exp(-slope[:, None] * r[None, :]).astype(np.float32)
    kd = np.exp(-slope[:, None] * (BS - r)[None, :]).astype(np.float32)
    diff = r[:, None] - r[None, :]
    dmat = -slope[:, None, None] * diff[None, :, :]
    dmat = np.where(diff[None] >= 0, dmat, -np.inf)
    dd = np.exp(dmat.astype(np.float32)).astype(np.float32)
    bd = np.exp(-slope * np.float32(BS)).astype(np.float64)
    return qd, kd, dd, [float(x) for x in bd]


class _Sem:
    __slots__ = ("name", "h", "n", "eng", "step")

    def __init__(self, name, eng, step):
        self.name, self.h, self.n, self.eng, self.step = name, None, 0, eng, step


class _Ev:
    __slots__ = ("sem", "val")

    def __init__(self, sem, val):
        self.sem, self.val = sem, val


class _Prog:

    def __init__(self):
        self.ops = {"T": [], "V": [], "S": [], "Y": [], "G": []}
        self.sems = []
        self.base = {}

    def sem(self, name, eng, step=1):
        s = _Sem(name, eng, step)
        self.sems.append(s)
        if eng in ("V", "S") and eng not in self.base:
            self.base[eng] = s
        return s

    def emit(self, eng, fn, waits=(), inc=None):
        w = [ww for ww in waits if ww is not None]
        if eng in ("V", "S"):
            b = self.base[eng]
            if b.n > 0:
                w.append(_Ev(b, b.n))
            if inc is None:
                inc = b
        w = tuple((ww.sem, ww.val) for ww in w)
        self.ops[eng].append((w, fn, inc))
        if inc is not None:
            inc.n += inc.step
            ev = _Ev(inc, inc.n)
            self.last_ev = getattr(self, "last_ev", {})
            self.last_ev[eng] = ev
            return ev
        return None

    def mark(self, eng, sem):
        w, fn, inc = self.ops[eng][-1]
        if inc is None:
            sem.n += sem.step
            self.ops[eng][-1] = (w, fn, sem)
            ev = _Ev(sem, sem.n)
            self.last_ev = getattr(self, "last_ev", {})
            self.last_ev[eng] = ev
            return ev
        return self.last_ev[eng]


def _build_program(stop=99, stop2_hp=15, fill_tiles=NT, fill_ln=True, fill_tr=True,
                   och_ocs=16):
    qd_t, kd_t, dd_t, bd_t = _decay_tables()
    nc = bass.Bass("TRN2", target_bir_lowering=False)
    P = _Prog()
    _build_program.last_prog = P

    xm_d = nc.dram_tensor("xm", [NT, 128, D], F32, kind="ExternalInput").ap()
    xp_d = nc.dram_tensor("xp", [NT, 128, D], F32, kind="ExternalInput").ap()
    wqk_d = nc.dram_tensor("wqk", [128, 128, 2048], F32, kind="ExternalInput").ap()
    wv_d = nc.dram_tensor("wv", [64, 128, 2048], F32, kind="ExternalInput").ap()
    wg_d = nc.dram_tensor("wg", [64, 128, 2048], F32, kind="ExternalInput").ap()
    wo_d = nc.dram_tensor("wo", [64, 128, 2048], F32, kind="ExternalInput").ap()
    wp_d = nc.dram_tensor("wp", [64, 128, 2048], F32, kind="ExternalInput").ap()
    wn_d = nc.dram_tensor("wn", [64, 128, 2048], F32, kind="ExternalInput").ap()
    nwb_d = nc.dram_tensor("nwb", [128, D], F32, kind="ExternalInput").ap()
    ddt_d = nc.dram_tensor("ddt", [H, 128, 2 * BS], F32, kind="ExternalInput").ap()
    qdkd_d = nc.dram_tensor("qdkd", [128, 128], F32, kind="ExternalInput").ap()
    idin_d = nc.dram_tensor("idin", [128, 128], F32, kind="ExternalInput").ap()
    out_d = nc.dram_tensor("out", [NT, 128, D], F32, kind="ExternalOutput").ap()
    xres_d = nc.dram_tensor("xres", [NT, 128, D], F32, kind="Internal").ap()
    xre2_d = nc.dram_tensor("xre2", [NT, 128, D], F32, kind="Internal").ap()

    stack = contextlib.ExitStack()
    pb_h = stack.enter_context(nc.sbuf_tensor("pb", [128, POOL_BF], BF16))
    pf_h = stack.enter_context(nc.sbuf_tensor("pf", [128, POOL_F32], F32))
    PS = {}
    for k in "ABCDEF":
        PS[k] = stack.enter_context(nc.psum_tensor("ps" + k, [128, 512], F32))
    PT = [stack.enter_context(nc.psum_tensor("pt0", [128, 512], BF16)),
          stack.enter_context(nc.psum_tensor("pt1", [128, 512], BF16))]

    def bf(off, n):
        return pb_h[:, off:off + n]

    def f32(off, n):
        return pf_h[:, off:off + n]

    def stc(i):
        return pf_h[:, ST_O + i:ST_O + i + 1]

    IDT = bf(IDT_O, 128)
    A3 = pb_h[:, A_OFF:A_OFF + 32768].rearrange("p (c t) -> p c t", t=R)
    PT3 = [PT[i][:].rearrange("p (a t) -> p a t", t=128) for i in range(2)]

    sT = P.sem("sT", "T")
    sV = P.sem("sV", "V")
    sS = P.sem("sS", "S")
    yA = P.sem("yA", "Y", 16)
    yA1 = P.sem("yA1", "Y", 16)
    yB = P.sem("yB", "Y", 16)
    yB2 = P.sem("yB2", "Y", 16)
    yD = P.sem("yD", "Y", 16)
    yI = P.sem("yI", "Y", 16)
    gN = P.sem("gN", "G", 16)
    g0 = P.sem("g0", "G", 16)
    g1 = P.sem("g1", "G", 16)
    g2 = P.sem("g2", "G", 16)
    g3 = P.sem("g3", "G", 16)
    g4 = P.sem("g4", "G", 16)
    g5 = P.sem("g5", "G", 16)
    gI = P.sem("gI", "G", 16)

    bank_free = {}
    free_ev = {}

    def mm(ps_ap, lhsT, rhs, start, stop, bank, waits=(), inc=None):
        w = list(waits)
        if start:
            w.append(bank_free.get(bank))
        return P.emit("T", lambda e, o=ps_ap, l=lhsT, r=rhs, st=start, sp=stop:
                      e.matmul(o, l, r, start=st, stop=sp), waits=w, inc=inc)

    def mark(eng):
        s = {"T": sT, "V": sV, "S": sS}[eng]
        return P.mark(eng, s)

    ev_qdkd = P.emit("Y", lambda e: e.dma_start(out=f32(QDKD_O, 128), in_=qdkd_d), inc=yI)
    ev_ident = [P.emit("G", lambda e: e.dma_start(out=IDT, in_=idin_d), inc=gI)]

    def qd_ap(h, qc):
        return pf_h[:, QDKD_O + h * 2 + qc: QDKD_O + h * 2 + qc + 1]

    def kd_ap(h, kc):
        return pf_h[:, QDKD_O + 64 + h * 2 + kc: QDKD_O + 64 + h * 2 + kc + 1]

    stats_busy = [None]

    def rstd_refine(col_var, col_out, eps, waits):
        evv = P.emit("V", lambda e: e.tensor_scalar_add(stc(8), col_var, float(eps)),
                     waits=waits, inc=sV)
        evs = P.emit("S", lambda e: e.activation(stc(9), stc(8), AF.Sqrt), waits=[evv], inc=sS)
        P.emit("V", lambda e: e.reciprocal(stc(10), stc(9)), waits=[evs])
        for _ in range(2):
            P.emit("V", lambda e: e.tensor_mul(stc(11), stc(10), stc(10)))
            P.emit("V", lambda e: e.tensor_mul(stc(11), stc(11), stc(8)))
            P.emit("V", lambda e: e.tensor_scalar(stc(11), stc(11), -0.5, 1.5, ALU.mult, ALU.add))
            P.emit("V", lambda e: e.tensor_mul(stc(10), stc(11), stc(10)))
        return P.emit("V", lambda e, co=col_out: e.tensor_copy(co, stc(10)), inc=sV)

    def row_sumsq(src_ap, out_col, waits):
        n = src_ap.shape[-1]
        n2 = n // 2
        e1 = P.emit("S", lambda e, s=src_ap: e.activation(bf(TRH_O, n2), s[:, 0:n2], AF.Square,
                                                          accum_out=stc(12)), waits=waits, inc=sS)
        e2 = P.emit("S", lambda e, s=src_ap: e.activation(bf(TRH_O, n2), s[:, n2:n], AF.Square,
                                                          accum_out=stc(13)), waits=waits, inc=sS)
        return P.emit("V", lambda e, oc=out_col: e.tensor_add(oc, stc(12), stc(13)),
                      waits=[e1, e2], inc=sV)

    def row_stats(src_ap, mean_col, var_col, waits):
        n = src_ap.shape[-1]
        ev_s = P.emit("V", lambda e, s=src_ap: e.reduce_sum(stc(14), s, axis=AX.X),
                      waits=waits, inc=sV)
        ev_q = row_sumsq(src_ap, stc(15), waits)
        P.emit("V", lambda e, mc=mean_col: e.tensor_scalar_mul(mc, stc(14), 1.0 / n), waits=[ev_s])
        P.emit("V", lambda e: e.tensor_scalar_mul(stc(15), stc(15), 1.0 / n), waits=[ev_q])
        P.emit("V", lambda e, mc=mean_col: e.tensor_mul(stc(14), mc, mc))
        return P.emit("V", lambda e, vc=var_col: e.tensor_sub(vc, stc(15), stc(14)), inc=sV)

    def transpose_tile_to_A(src_off, tt, src_ready, extra_first_waits=()):
        last = None
        for cg in range(8):
            pi = cg % 2
            evt = None
            for ci in range(4):
                c = cg * 4 + ci
                w = [src_ready, ev_ident[-1]]
                if ci == 0:
                    w.append(bank_free.get(("pt", pi)))
                w += list(extra_first_waits) if (cg == 0 and ci == 0) else []
                evt = P.emit("T", lambda e, cc=c, b=pi, ii=ci, so=src_off: e.transpose(
                    PT[b][:, ii * 128:(ii + 1) * 128],
                    bf(so + cc * 128, 128), IDT),
                    waits=w, inc=sT if ci == 3 else None)
            ev_ev = P.emit("S", lambda e, cg_=cg, t=tt, b=pi: e.activation(
                A3[:, cg_ * 4:cg_ * 4 + 4, t * 128:(t + 1) * 128],
                PT3[b][:, 0:4, :], AF.Copy),
                waits=[evt], inc=sS)
            bank_free[("pt", pi)] = ev_ev
            last = ev_ev
        return last

    def emit_fill_A(src_d, a_readers_done, slots=(XF1,)):
        last = None
        for tt in range(fill_tiles):
            sl = tt % 2
            xfo = slots[tt % len(slots)]
            ev_ld = P.emit("Y", lambda e, t=tt, o=xfo: e.dma_start(
                out=f32(o, 4096), in_=src_d[t]),
                waits=[free_ev.get(("xf", xfo))], inc=(yA if xfo == XF1 else yA1))
            if not fill_ln:
                last = ev_ld
                free_ev[("xf", xfo)] = ev_ld
                continue
            xf = f32(xfo, 4096)
            ev_var = row_stats(xf, stc(0), stc(1), [ev_ld, stats_busy[0]])
            ev_rstd = rstd_refine(stc(1), stc(2), EPS, [ev_var])
            ev_ap = P.emit("V", lambda e, s=sl, o=xfo: e.tensor_scalar(
                bf(X1T0 + s * 4096, 4096), f32(o, 4096), stc(0), stc(2),
                ALU.subtract, ALU.mult),
                waits=[free_ev.get(("x1t", sl)), ev_rstd], inc=sV)
            stats_busy[0] = ev_ap
            free_ev[("xf", xfo)] = ev_ap
            if not fill_tr:
                last = ev_ap
                free_ev[("x1t", sl)] = ev_ap
                continue
            last = transpose_tile_to_A(X1T0 + sl * 4096, tt, ev_ap,
                                       extra_first_waits=[a_readers_done])
            free_ev[("x1t", sl)] = last
        return last

    wslot_i = [0]
    wsem = [g0, g1]

    def load_wslot(src_ap):
        i = wslot_i[0] % 2
        wslot_i[0] += 1
        ev = P.emit("G", lambda e, ii=i, s=src_ap: e.dma_start(
            out=bf(WQK0 + ii * 2048, 2048), in_=s),
            waits=[free_ev.get(("wslot", i))], inc=wsem[i])
        return i, ev

    def proj_fm(g, banks, a_ready):
        fills = [None, None]
        for half in range(2):
            sl, ev_ld = load_wslot(wqk_d[2 * g + half])
            last = None
            for dkl in range(16):
                dk = half * 16 + dkl
                for th in range(2):
                    last = mm(PS[banks[th]][:, 0:512],
                              bf(WQK0 + sl * 2048 + dkl * 128, 128),
                              bf(A_OFF + dk * R + th * 512, 512),
                              start=(dk == 0), stop=(dk == 31), bank=banks[th],
                              waits=[ev_ld, a_ready],
                              inc=sT if dkl == 15 else None)
                    if dk == 31:
                        fills[th] = last
            free_ev[("wslot", sl)] = last
        return fills

    def emit_headpair(hp, prefix, a_ready, kv_init_ev):
        for hh in range(2):
            h = 2 * hp + hh
            hs = h % 2
            qb = QB0 if hs == 0 else QB1
            kf = KF0 if hs == 0 else KF1
            kt = KT0 if hs == 0 else KT1
            ddo = DD0 if hs == 0 else DD1
            if not prefix:
                ev_dd = P.emit("G", lambda e, h_=h, d_=ddo: e.dma_start(
                    out=bf(d_, 512), in_=ddt_d[h_]),
                    waits=[free_ev.get(("dd", hs))], inc=(g2 if hs == 0 else g3))
                free_ev[("ddld", h)] = ev_dd
                fills = proj_fm(h, ("A", "B"), a_ready)
                for th in range(2):
                    ev = P.emit("S", lambda e, t=th, q=qb, b=("A", "B")[th]: e.activation(
                        bf(q + t * 512, 512), PS[b][:, 0:512], AF.Silu),
                        waits=[fills[th], free_ev.get(("qb", hs))], inc=sS)
                    bank_free[("A", "B")[th]] = ev
                    free_ev[("qrdy", h, th)] = ev
            fills = proj_fm(32 + h, ("C", "D"), a_ready)
            for th in range(2):
                ev = P.emit("S", lambda e, t=th, k=kf, b=("C", "D")[th]: e.activation(
                    bf(k + t * 512, 512), PS[b][:, 0:512], AF.Silu),
                    waits=[fills[th], free_ev.get(("kf", hs))], inc=sS)
                bank_free[("C", "D")[th]] = ev
                free_ev[("krdy", h, th)] = ev
            for tt in range(NT):
                pi = tt % 2
                w = [free_ev[("krdy", h, tt // 4)], ev_ident[-1], bank_free.get(("pt", pi))]
                if tt < 2:
                    w.append(free_ev.get(("kt", hs)))
                evt = P.emit("T", lambda e, k=kf, t=tt, b=pi: e.transpose(
                    PT[b][:, 0:128],
                    bf(k + t * 128, 128), IDT), waits=w, inc=sT)
                ev_ev = P.emit("S", lambda e, t=tt, k2=kt, b=pi, h_=h: e.activation(
                    bf(k2 + t * 128, 128),
                    PT[b][:, 0:128], AF.Copy,
                    scale=kd_ap(h_, t % 2)),
                    waits=[evt, ev_qdkd], inc=sS)
                bank_free[("pt", pi)] = ev_ev
                free_ev[("ktm", h, tt)] = ev_ev

        vto = VT0
        for p2 in range(2):
            vlast = None
            for sl4 in range(4):
                sl, ev_ld = load_wslot(wv_d[hp * 4 + sl4])
                for a in range(8):
                    dk = sl4 * 8 + a
                    for t4 in range(4):
                        tt = p2 * 4 + t4
                        bk = "ABCD"[t4]
                        vlast = mm(PS[bk][:, 0:256],
                                   bf(A_OFF + dk * R + tt * 128, 128),
                                   bf(WQK0 + sl * 2048 + a * 256, 256),
                                   start=(dk == 0), stop=(dk == 31), bank=bk,
                                   waits=[ev_ld, a_ready],
                                   inc=sT if (a == 7 and t4 == 3) else None)
                free_ev[("wslot", sl)] = vlast
            for t4 in range(4):
                tt = p2 * 4 + t4
                bk = "ABCD"[t4]
                ev = P.emit("S", lambda e, t=tt, b=bk, v=vto: e.activation(
                    bf(v + t * 256, 256), PS[b][:, 0:256], AF.Silu),
                    waits=[vlast, free_ev.get(("vt",))], inc=sS)
                free_ev[("vrdy", hp, tt)] = ev
                bank_free[bk] = ev

        for hh in range(2):
            h = 2 * hp + hh
            hs = h % 2
            qb = QB0 if hs == 0 else QB1
            kf = KF0 if hs == 0 else KF1
            kt = KT0 if hs == 0 else KT1
            ddo = DD0 if hs == 0 else DD1
            kvo = KVA_O + hp * 256 + hh * 128
            kv = f32(kvo, 128)
            last_sdt = [None, None]
            last_inter = None
            last_state = None
            for n in range(NBLK):
                if not prefix:
                    w = [kv_init_ev, free_ev.get("kvb")]
                    ev_kvb = P.emit("V", lambda e, k=kv: e.tensor_copy(bf(KVB_O, 128), k),
                                    waits=w, inc=sV)
                    ev_sd = [None, None]
                    for ktc in range(2):
                        tile = 2 * n + ktc
                        bk = "EF"[ktc]
                        evs_ = mm(PS[bk][:, 0:256], bf(kf + tile * 128, 128),
                                  bf(qb + n * 256, 256), start=True, stop=True, bank=bk,
                                  waits=[free_ev[("krdy", h, tile // 4)],
                                         free_ev[("qrdy", h, n // 2)]],
                                  inc=sT)
                        ev_sd[ktc] = P.emit("V", lambda e, k2=ktc, b=bk, h_=h: e.tensor_tensor(
                            bf(SDT_O + k2 * 256, 256), PS[b][:, 0:256],
                            bf((DD0 if h_ % 2 == 0 else DD1) + k2 * 256, 256), ALU.mult),
                            waits=[evs_, free_ev[("ddld", h)], free_ev.get("sdt")], inc=sV)
                        bank_free[bk] = ev_sd[ktc]
                    ev_emain = [None, None]
                    for qtc in range(2):
                        tt = 2 * n + qtc
                        bmain = "CD"[qtc]
                        binter = "AB"[qtc]
                        evm = None
                        for ktc in range(2):
                            evm = mm(PS[bmain][:, 0:128],
                                     bf(SDT_O + ktc * 256 + qtc * 128, 128),
                                     bf(vto + (2 * n + ktc) * 256 + hh * 128, 128),
                                     start=(ktc == 0), stop=(ktc == 1), bank=bmain,
                                     waits=[ev_sd[ktc], free_ev[("vrdy", hp, 2 * n + ktc)]],
                                     inc=sT if ktc == 1 else None)
                        evi = mm(PS[binter][:, 0:128], bf(qb + tt * 128, 128),
                                 bf(KVB_O, 128), start=True, stop=True, bank=binter,
                                 waits=[ev_kvb, free_ev[("qrdy", h, n // 2)]], inc=sT)
                        boff = B_OFF + tt * 4096 + h * 128
                        P.emit("V", lambda e, bo=boff, b=binter, h_=h, q=qtc: e.tensor_scalar_mul(
                            bf(bo, 128), PS[b][:, 0:128], qd_ap(h_, q)),
                            waits=[evi, ev_qdkd])
                        eva = P.emit("V", lambda e, bo=boff, b=bmain: e.tensor_add(
                            bf(bo, 128), bf(bo, 128), PS[b][:, 0:128]),
                            waits=[evm], inc=sV)
                        bank_free[bmain] = eva
                        bank_free[binter] = eva
                        ev_emain[qtc] = eva
                        last_inter = eva
                    free_ev["sdt"] = ev_emain[1]
                    free_ev["kvb"] = _last_T_inter = None
                    free_ev["kvb"] = ev_emain[1]
                evst = None
                for ktc in range(2):
                    tile = 2 * n + ktc
                    evst = mm(PS["F"][:, 128:256], bf(kt + tile * 128, 128),
                              bf(vto + tile * 256 + hh * 128, 128),
                              start=(ktc == 0), stop=(ktc == 1), bank="F",
                              waits=[free_ev[("ktm", h, tile)], free_ev[("vrdy", hp, tile)]],
                              inc=sT if ktc == 1 else None)
                w = [evst, kv_init_ev]
                ev_upd = P.emit("V", lambda e, k=kv, b=bd_t[h]: e.scalar_tensor_tensor(
                    k, k, b, PS["F"][:, 128:256], ALU.mult, ALU.add), waits=w, inc=sV)
                bank_free["F"] = ev_upd
                last_state = ev_upd
            free_ev[("qb", hs)] = last_inter if not prefix else None
            free_ev[("kf", hs)] = last_inter if not prefix else last_state
            free_ev[("kt", hs)] = last_state
            free_ev[("dd", hs)] = last_inter if not prefix else None
            free_ev[("kvdone", h)] = last_state
        free_ev[("vt",)] = free_ev[("kvdone", 2 * hp + 1)]

    def emit_och_matmul(w_dram, a_ready, evac_fn, first_w=(), first_w_odd=None,
                        name=""):
        grp = [("A", "B", "C"), ("D", "E", "F")]
        last_evs = []
        for oc in range(och_ocs):
            wbuf = WOCH0 if oc % 2 == 0 else WOCH1
            gsem = g2 if oc % 2 == 0 else g3
            fw = list(first_w) if (oc % 2 == 0 or first_w_odd is None) else list(first_w_odd)
            evw = None
            for piece in range(4):
                evw = P.emit("G", lambda e, s=w_dram[oc * 4 + piece], wb=wbuf, pp=piece: e.dma_start(
                    out=bf(wb + pp * 2048, 2048), in_=s),
                    waits=[free_ev.get(("woch", oc % 2))] + (fw if piece == 0 else []),
                    inc=gsem)
            banks = grp[oc % 2]
            lastmm = None
            for tt in range(NT):
                bk = banks[tt % 3]
                for dk in range(DC):
                    lastmm = mm(PS[bk][:, 0:256],
                                bf(A_OFF + dk * R + tt * 128, 128),
                                bf(wbuf + (dk // 8) * 2048 + (dk % 8) * 256, 256),
                                start=(dk == 0), stop=(dk == DC - 1),
                                bank=(bk if dk == 0 else None),
                                waits=[evw, a_ready],
                                inc=sT if dk == DC - 1 else None)
                ev = evac_fn(oc, tt, PS[bk][:, 0:256], [lastmm])
                bank_free[bk] = ev
                last_evs.append(ev)
            free_ev[("woch", oc % 2)] = lastmm
        return last_evs[-1]

    def _materialize():
        for s in P.sems:
            s.h = stack.enter_context(nc.semaphore(s.name))

        with nc.Block() as block:
            def runner(engname):
                def run(e):
                    hw = {}
                    for waits, fn, inc in P.ops[engname]:
                        for sem, val in waits:
                            if hw.get(sem.name, 0) < val:
                                e.wait_ge(sem.h, val)
                                hw[sem.name] = val
                        inst = fn(e)
                        if inc is not None:
                            inst.then_inc(inc.h, inc.step)
                return run

            block.tensor(runner("T"))
            block.vector(runner("V"))
            block.scalar(runner("S"))
            block.sync(runner("Y"))
            block.gpsimd(runner("G"))

        stack.close()
        return nc

    def _early_out(last_ev):
        for tt in range(NT):
            P.emit("Y", lambda e, t=tt: e.dma_start(out=out_d[t], in_=f32(XF0, 4096)),
                   waits=[last_ev], inc=yD)
        P.emit("G", lambda e: e.wait_ge(yD.h, yD.n), waits=())
        return _materialize()

    ev_a1 = emit_fill_A(xp_d, None, slots=(XF1, KVA_O))
    if stop <= 1:
        return _early_out(ev_a1)
    free_ev[("wslot", 0)] = ev_a1
    free_ev[("wslot", 1)] = ev_a1
    prefix_done = {}
    for hp in range(16):
        ev_init = P.emit("V", lambda e, o=KVA_O + hp * 256: e.memset(f32(o, 256), 0.0),
                         inc=sV)
        emit_headpair(hp, prefix=True, a_ready=ev_a1, kv_init_ev=ev_init)
        prefix_done[hp] = free_ev[("kvdone", 2 * hp + 1)]
        if stop <= 2 and hp >= stop2_hp:
            break
    ev_prefix_T = mark("T")
    if stop <= 2:
        return _early_out(mark("V"))

    ev_a2 = emit_fill_A(xm_d, ev_prefix_T)
    if stop <= 3:
        return _early_out(ev_a2)
    free_ev[("wslot", 0)] = ev_a2
    free_ev[("wslot", 1)] = ev_a2

    for hp in range(16):
        emit_headpair(hp, prefix=False, a_ready=ev_a2, kv_init_ev=prefix_done[hp])
        if stop <= 4 and hp >= stop2_hp:
            break
    ev_attn_T = mark("T")
    ev_attn_V = mark("V")
    if stop <= 4:
        return _early_out(ev_attn_V)

    ev_nw = P.emit("G", lambda e: e.dma_start(out=bf(NW_O, 4096), in_=nwb_d),
                   waits=[ev_attn_T], inc=gN)
    ev_norm_last = None
    for tt in range(NT):
        bt = bf(B_OFF + tt * 4096, 4096)
        ev_q = row_sumsq(bt, stc(0), [ev_attn_V, stats_busy[0]])
        P.emit("V", lambda e: e.tensor_scalar_mul(stc(0), stc(0), 1.0 / D), waits=[ev_q])
        ev_r = rstd_refine(stc(0), stc(1), EPS, [])
        ev1 = P.emit("V", lambda e, b=bt: e.tensor_scalar_mul(b, b, stc(1)), waits=[ev_r], inc=sV)
        stats_busy[0] = ev1
        ev_norm_last = P.emit("V", lambda e, b=bt: e.tensor_mul(b, b, bf(NW_O, 4096)),
                              waits=[ev_nw], inc=sV)
    if stop <= 5:
        return _early_out(ev_norm_last)

    def gate_evac(oc, tt, ps_ap, waits):
        gsl = GST0 if tt % 2 == 0 else GST1
        evs_ = P.emit("S", lambda e, g=gsl, p=ps_ap: e.activation(bf(g, 256), p, AF.Sigmoid),
                      waits=waits + [free_ev.get(("gst", tt % 2)), ev_norm_last], inc=sS)
        bo = B_OFF + tt * 4096 + oc * 256
        ev = P.emit("V", lambda e, b=bo, g=gsl: e.tensor_mul(bf(b, 256), bf(b, 256), bf(g, 256)),
                    waits=[evs_], inc=sV)
        free_ev[("gst", tt % 2)] = ev
        return ev

    ev_gate_last = emit_och_matmul(wg_d, ev_a2, gate_evac,
                                   first_w=[ev_attn_T],
                                   first_w_odd=[ev_attn_T, ev_norm_last], name="gate")
    ev_gate_T = mark("T")
    if stop <= 6:
        return _early_out(ev_gate_last)

    ev_ident.append(P.emit("G", lambda e: e.dma_start(out=IDT, in_=idin_d),
                           waits=[ev_gate_T], inc=gI))
    ev_ng = None
    for tt in range(NT):
        ev_ng = transpose_tile_to_A(B_OFF + tt * 4096, tt, ev_gate_last,
                                    extra_first_waits=[ev_gate_T])
    ev_ng_T = mark("T")
    if stop <= 7:
        return _early_out(ev_ng)

    def y_evac(oc, tt, ps_ap, waits):
        bo = B_OFF + tt * 4096 + oc * 256
        return P.emit("S", lambda e, b=bo, p=ps_ap: e.activation(bf(b, 256), p, AF.Copy),
                      waits=waits + [ev_ng_T if oc == 0 else None], inc=sS)

    ev_y_last = emit_och_matmul(wo_d, ev_ng, y_evac, first_w=[ev_ng_T], name="out")
    ev_out_T = mark("T")
    if stop <= 8:
        return _early_out(ev_y_last)

    TMP = f32(KVA_O, 4096)
    for tt in range(NT):
        bt = bf(B_OFF + tt * 4096, 4096)
        ev_var = row_stats(bt, stc(0), stc(1), [ev_y_last, stats_busy[0]])
        ev_r = rstd_refine(stc(1), stc(2), EPS, [ev_var])
        ev_t = P.emit("V", lambda e, b=bt: e.tensor_scalar(
            f32(XF1, 4096), b, stc(0), stc(2), ALU.subtract, ALU.mult),
            waits=[ev_r, free_ev.get(("xf", XF1))], inc=sV)
        ev_hi = None
        for h2 in range(2):
            off = h2 * 2048
            ev_ldx = P.emit("Y", lambda e, t=tt, o=off: e.dma_start(
                out=f32(XF0H, 2048), in_=xm_d[t][:, o:o + 2048]),
                waits=[free_ev.get(("xf8h",)) or free_ev.get(("xf", XF1))], inc=yA)
            ev_hi = P.emit("V", lambda e, o=off: e.tensor_add(
                pf_h[:, KVA_O + o:KVA_O + o + 2048], f32(XF1 + o, 2048), f32(XF0H, 2048)),
                waits=[ev_ldx, ev_t, free_ev.get(("xfhi",)) if h2 == 0 else None], inc=sV)
            free_ev[("xf8h",)] = ev_hi
        ev_sh = P.emit("Y", lambda e, t=tt: e.dma_start(out=xres_d[t], in_=TMP),
                       waits=[ev_hi], inc=yB)
        free_ev[("xres", tt)] = ev_sh
        free_ev[("xfhi",)] = ev_sh
        ev_var2 = row_stats(TMP, stc(3), stc(4), [ev_hi])
        ev_r2 = rstd_refine(stc(4), stc(5), EPS, [ev_var2])
        ev_x2 = P.emit("V", lambda e, b=bt: e.tensor_scalar(
            b, TMP, stc(3), stc(5), ALU.subtract, ALU.mult),
            waits=[ev_r2], inc=sV)
        stats_busy[0] = ev_x2
        free_ev[("x2", tt)] = ev_x2
    ev_x2_all = mark("V")
    if stop <= 9:
        return _early_out(ev_x2_all)

    ev_ident.append(P.emit("G", lambda e: e.dma_start(out=IDT, in_=idin_d),
                           waits=[ev_out_T], inc=gI))
    ev_x2fm = None
    for tt in range(NT):
        ev_x2fm = transpose_tile_to_A(B_OFF + tt * 4096, tt, free_ev[("x2", tt)],
                                      extra_first_waits=[ev_out_T])
    ev_x2fm_T = mark("T")
    if stop <= 10:
        return _early_out(ev_x2fm)

    wpn_sem = {0: g2, 1: g3, 2: g4, 3: g5}
    wd_ready = {}
    for oc in range(16):
        for sl4 in range(4):
            sp = WPN0 if sl4 % 2 == 0 else WPN1
            sn = WPN2 if sl4 % 2 == 0 else WPN3
            idx = sl4 % 2
            evp = P.emit("G", lambda e, s=sp, src=wp_d[oc * 4 + sl4]: e.dma_start(
                out=bf(s, 2048), in_=src),
                waits=[free_ev.get(("wpn", idx)), ev_x2fm_T if (oc == 0 and sl4 == 0) else None],
                inc=wpn_sem[idx])
            evn = P.emit("G", lambda e, s=sn, src=wn_d[oc * 4 + sl4]: e.dma_start(
                out=bf(s, 2048), in_=src),
                waits=[free_ev.get(("wpn2", idx))], inc=wpn_sem[idx + 2])
            ev1 = P.emit("V", lambda e, s=sp, q=sl4: e.tensor_scalar(
                bf(WD_O + q * 2048, 2048), bf(s, 2048), 0.0, None, ALU.is_gt),
                waits=[evp, free_ev.get(("wd", sl4))], inc=sV)
            free_ev[("wpn", idx)] = ev1
            ev2 = P.emit("V", lambda e, s=sn: e.tensor_scalar(
                bf(s, 2048), bf(s, 2048), 0.0, None, ALU.is_gt),
                waits=[evn], inc=sV)
            ev3 = P.emit("V", lambda e, s=sn, q=sl4: e.tensor_sub(
                bf(WD_O + q * 2048, 2048), bf(WD_O + q * 2048, 2048), bf(s, 2048)),
                waits=[ev1, ev2], inc=sV)
            free_ev[("wpn2", idx)] = ev3
            wd_ready[sl4] = ev3
        banks = [("A", "B", "C"), ("D", "E", "F")][oc % 2]
        lastmm = None
        for tt in range(NT):
            bk = banks[tt % 3]
            for dk in range(DC):
                lastmm = mm(PS[bk][:, 0:256],
                            bf(A_OFF + dk * R + tt * 128, 128),
                            bf(WD_O + (dk // 8) * 2048 + (dk % 8) * 256, 256),
                            start=(dk == 0), stop=(dk == DC - 1),
                            bank=(bk if dk == 0 else None),
                            waits=[wd_ready[dk // 8], ev_x2fm if (dk == 0 and tt == 0) else None],
                            inc=sT if dk == DC - 1 else None)
            bo = B_OFF + tt * 4096 + oc * 256
            ev = P.emit("S", lambda e, b=bo, ps=PS[bk]: e.activation(
                bf(b, 256), ps[:, 0:256], AF.Copy),
                waits=[lastmm, ev_x2_all if oc == 0 else None], inc=sS)
            bank_free[bk] = ev
        for q in range(4):
            free_ev[("wd", q)] = lastmm
    ev_ffn_T = mark("T")
    ev_ffn_V = mark("V")
    if stop <= 11:
        return _early_out(ev_ffn_V)

    ph8_x2 = stats_busy[0]
    for tt in range(NT):
        bt = bf(B_OFF + tt * 4096, 4096)
        ev_var = row_stats(bt, stc(0), stc(1), [ev_ffn_V, ev_ffn_T, stats_busy[0]])
        ev_r = rstd_refine(stc(1), stc(2), EPS, [ev_var])
        ev_t = P.emit("V", lambda e, b=bt: e.tensor_scalar(
            f32(XF1, 4096), b, stc(0), stc(2), ALU.subtract, ALU.mult),
            waits=[ev_r, free_ev.get(("xf11", 1))], inc=sV)
        ev_o = None
        for h2 in range(2):
            off = h2 * 2048
            ev_ldh = P.emit("Y", lambda e, t=tt, o=off: e.dma_start(
                out=f32(XF0H, 2048), in_=xres_d[t][:, o:o + 2048]),
                waits=[free_ev[("xres", tt)], free_ev.get(("xf8h",))], inc=yA)
            ev_o = P.emit("V", lambda e, o=off: e.tensor_add(
                f32(XF1 + o, 2048), f32(XF1 + o, 2048), f32(XF0H, 2048)),
                waits=[ev_ldh, ev_t], inc=sV)
            free_ev[("xf8h",)] = ev_o
        ev_st = P.emit("Y", lambda e, t=tt: e.dma_start(out=out_d[t], in_=f32(XF1, 4096)),
                       waits=[ev_o], inc=yD)
        free_ev[("xf11", 1)] = ev_st

    P.emit("G", lambda e: e.wait_ge(yD.h, yD.n), waits=())
    return _materialize()


_CACHE = {}


def _pack_och(wT):
    return np.ascontiguousarray(
        wT.reshape(4, 8, 128, 16, 256).transpose(3, 0, 2, 1, 4).reshape(64, 128, 2048))


def _host_prep(inputs):
    hs = np.asarray(inputs["L0_attn_hidden_states"], dtype=np.float32)
    qkv_w = np.asarray(inputs["L0_attn_qkv_w"], dtype=np.float32)
    out_w = np.asarray(inputs["L0_attn_out_w"], dtype=np.float32)
    gate_w = np.asarray(inputs["L0_attn_gate_w"], dtype=np.float32)
    norm_w = np.asarray(inputs["L0_attn_norm_w"], dtype=np.float32)
    w_pos = np.asarray(inputs["L0_ffn_w_pos"], dtype=np.float32)
    w_neg = np.asarray(inputs["L0_ffn_w_neg"], dtype=np.float32)

    wq = np.ascontiguousarray(
        qkv_w.reshape(H, 3, HD, D).transpose(1, 0, 2, 3).reshape(3 * D, D).T)
    wqk = np.ascontiguousarray(
        wq[:, :2 * D].reshape(2, 16, 128, 64, 128).transpose(3, 0, 2, 1, 4).reshape(
            128, 128, 2048))
    wv = _pack_och(wq[:, 2 * D:])
    wg = _pack_och(np.ascontiguousarray(gate_w.T))
    wo = _pack_och(np.ascontiguousarray(out_w.T))
    wp = _pack_och(np.ascontiguousarray(w_pos.T))
    wn = _pack_och(np.ascontiguousarray(w_neg.T))
    nwb = np.ascontiguousarray(np.broadcast_to(norm_w[None, :], (128, D)))

    qd_t, kd_t, dd_t, _bd = _decay_tables()
    ddt = np.ascontiguousarray(
        dd_t.reshape(H, BS, 2, 128).transpose(0, 3, 2, 1).reshape(H, 128, 2 * BS))
    qdkd = np.zeros((128, 128), dtype=np.float32)
    for h in range(H):
        for c in range(2):
            qdkd[:, h * 2 + c] = qd_t[h, c * 128:(c + 1) * 128]
            qdkd[:, 64 + h * 2 + c] = kd_t[h, c * 128:(c + 1) * 128]
    idin = np.eye(128, dtype=np.float32)

    zeros_half = np.zeros((NT, 128, D), dtype=np.float32)
    in_maps = []
    for c in range(8):
        b, j = c // 2, c % 2
        xm = np.ascontiguousarray(hs[b, j * R:(j + 1) * R].reshape(NT, 128, D))
        xp = zeros_half if j == 0 else np.ascontiguousarray(hs[b, 0:R].reshape(NT, 128, D))
        in_maps.append(dict(xm=xm, xp=xp, wqk=wqk, wv=wv, wg=wg, wo=wo, wp=wp, wn=wn,
                            nwb=nwb, ddt=ddt, qdkd=qdkd, idin=idin))
    return in_maps


def kernel(**inputs):
    if "nc" not in _CACHE:
        _CACHE["nc"] = _build_program()
    nc = _CACHE["nc"]
    in_maps = _host_prep(inputs)
    res = run_bass_kernel_spmd(nc, in_maps, core_ids=list(range(8)))
    out = np.empty((4, 2048, D), dtype=np.float32)
    for c in range(8):
        b, j = c // 2, c % 2
        out[b, j * R:(j + 1) * R] = np.asarray(res.results[c]["out"]).reshape(R, D)
    return out
